# version 11
# baseline (speedup 1.0000x reference)
import numpy as np
import ml_dtypes
from contextlib import ExitStack
import concourse.bass as bass
import concourse.mybir as mybir

F32 = mybir.dt.float32
BF16 = mybir.dt.bfloat16
AF = mybir.ActivationFunctionType
AX = mybir.AxisListType
OP = mybir.AluOpType
BF = ml_dtypes.bfloat16

D = 4096
T = 2048
TOWN = 1024
NH = 32
NKV = 8
HD = 128
NF = D // 128
SCALE = float(1.0 / np.sqrt(np.float32(HD)))
RMS_EPS = 1.1920929e-07
LN_EPS = 1e-5
SINK_B = 24.0
NEG = -1e30
JSETS = [[0, 2, 5, 7], [1, 3, 4, 6]]


class Sem:
    def __init__(self, stack, nc, name):
        self.h = stack.enter_context(nc.semaphore(name))
        self.n = 0

    def inc(self, instr, k=1):
        instr.then_inc(self.h, k)
        self.n += k
        return self.n


class Ctx:
    pass


def build(debug_taps=(), upto="all"):
    nc = bass.Bass("TRN2", target_bir_lowering=False, debug=False)
    c = Ctx()
    c.nc = nc
    c.taps = set(debug_taps)

    c.xb = nc.dram_tensor("xb", [T, D], F32, kind="ExternalInput").ap()
    c.xq = nc.dram_tensor("xq", [TOWN, D], F32, kind="ExternalInput").ap()
    c.wq = nc.dram_tensor("wq", [D, D], F32, kind="ExternalInput").ap()
    c.wkv = nc.dram_tensor("wkv", [D, 2048], F32, kind="ExternalInput").ap()
    c.wo = nc.dram_tensor("wo", [D, D], F32, kind="ExternalInput").ap()
    c.wbr = nc.dram_tensor("wbr", [1024, 1024], F32, kind="ExternalInput").ap()
    c.wlog = nc.dram_tensor("wlog", [D, 24], F32, kind="ExternalInput").ap()
    c.cosf = nc.dram_tensor("cosf", [128, T], F32, kind="ExternalInput").ap()
    c.sinf = nc.dram_tensor("sinf", [128, T], F32, kind="ExternalInput").ap()
    c.coso = nc.dram_tensor("coso", [128, TOWN], F32, kind="ExternalInput").ap()
    c.sino = nc.dram_tensor("sino", [128, TOWN], F32, kind="ExternalInput").ap()
    c.maskq = nc.dram_tensor("maskq", [128, 4096], F32, kind="ExternalInput").ap()
    c.ident = nc.dram_tensor("ident", [128, 128], BF16, kind="ExternalInput").ap()
    c.onesb = nc.dram_tensor("onesb", [128, 128], BF16, kind="ExternalInput").ap()
    c.hbase = nc.dram_tensor("hbase", [24, 1], F32, kind="ExternalInput").ap()
    c.gate = nc.dram_tensor("gate", [1, 1], F32, kind="ExternalInput").ap()
    c.taub = nc.dram_tensor("taub", [24, 1], F32, kind="ExternalInput").ap()
    c.idf = nc.dram_tensor("idf", [128, 128], F32, kind="ExternalInput").ap()
    c.onesf = nc.dram_tensor("onesf", [1, 128], F32, kind="ExternalInput").ap()
    c.epsv = nc.dram_tensor("epsv", [128, 2], F32, kind="ExternalInput").ap()
    c.out = nc.dram_tensor("out", [TOWN, D], F32, kind="ExternalOutput").ap()
    c.dbg = {}
    for name, shape in [("Q16", [128, NH * TOWN]), ("K16", [128, NKV * T]),
                        ("V16", [128, 16 * 1024]), ("AO", [128, NH * TOWN]),
                        ("U", [TOWN, D]), ("X1", [TOWN, D]),
                        ("LOG", [TOWN, 32]), ("SNK", [128, 8 * 32]),
                        ("Y", [TOWN, 1024])]:
        if name in c.taps:
            dt = BF16 if name in ("Q16", "K16", "V16", "AO") else F32
            c.dbg[name] = nc.dram_tensor("dbg_" + name, shape, dt,
                                         kind="ExternalOutput").ap()

    with ExitStack() as st:
        c.st = st
        for s in ["gd", "sd", "cd", "pe", "dv", "ac", "po", "ex",
                  "gdx0", "gdx1", "sdx0", "sdx1", "gdw0", "gdw1", "gdw2", "gdw3", "sdr", "sdt", "sdt1", "sdp"]:
            setattr(c, s, Sem(st, nc, "s_" + s))
        c.kw_done = {}
        c.mm_done = {}
        c.x16_cast_done = {}
        c.x16_tp_done = {}
        c.wbuf_guard = None
        c.rope_scr_guard = None
        c.rope_rtmp_guard = None
        c.xres_used = {}
        c.logit_mm = {}
        c.zrow_used = 0
        c.zsb_used = 0
        c.y_mm = 0
        c.ysb_ac = 0
        c.xp_po_used = None
        c.last_out_sd = None
        c.xr_ok = {}
        c.xn_dv = {}

        c.blk = st.enter_context(nc.Block())
        c.idsb = st.enter_context(nc.sbuf_tensor("idsb", [128, 128], BF16))
        c.onsb = st.enter_context(nc.sbuf_tensor("onsb", [128, 128], BF16))
        c.hbsb = st.enter_context(nc.sbuf_tensor("hbsb", [24, 1], F32))
        c.gtsb = st.enter_context(nc.sbuf_tensor("gtsb", [1, 1], F32))
        c.tbsb = st.enter_context(nc.sbuf_tensor("tbsb", [24, 1], F32))
        c.idfsb = st.enter_context(nc.sbuf_tensor("idfsb", [128, 128], F32))
        c.onfsb = st.enter_context(nc.sbuf_tensor("onfsb", [1, 128], F32))
        c.epsb = st.enter_context(nc.sbuf_tensor("epsb", [128, 2], F32))
        c.scr = st.enter_context(nc.sbuf_tensor("scr", [128, 1024], BF16))

        phase_consts(c)
        c.kvst = ExitStack()
        c.K16 = c.kvst.enter_context(nc.sbuf_tensor("K16", [128, NKV, T], BF16))
        phase_kv(c)
        c.qst = ExitStack()
        c.Q16 = c.qst.enter_context(nc.sbuf_tensor("Q16", [128, NH, TOWN], BF16, side="right"))
        phase_q(c)
        c.V16 = c.kvst.enter_context(nc.sbuf_tensor("V16", [128, 16, 1024], BF16))
        phase_v(c)
        emit_taps_p1(c)
        if upto == "p1":
            finish_dev(c)
            c.qst.close(); c.kvst.close()
            return nc, c
        phase_attn(c)
        if upto == "p2":
            finish_dev(c)
            c.qst.close(); c.kvst.close()
            return nc, c
        phase_tail(c)
        c.qst.close()
    return nc, c


def finish_dev(c):
    nc = c.nc
    if (blk := c.blk):
        @blk.sync
        def _(sy):
            for s in [c.gd, c.sd, c.cd, c.pe, c.dv, c.ac, c.po, c.ex,
                      c.gdx0, c.gdx1, c.sdx0, c.sdx1, c.gdw0, c.gdw1, c.gdw2, c.gdw3, c.sdr, c.sdt, c.sdt1, c.sdp]:
                if s.n:
                    sy.wait_ge(s.h, s.n)


def phase_consts(c):
    nc = c.nc
    if (blk := c.blk):
        @blk.scalar
        def _(sc):
            from concourse.hw_specs import get_activation_tables
            for idx, funcs in enumerate(
                    get_activation_tables(nc.m.arch).values()):
                if AF.Exp in funcs and AF.Ln in funcs:
                    sc.add_instruction(mybir.InstLoadActFuncSet(
                        name=nc.get_next_instruction_name(),
                        act_func_set_id=idx, ins=[], outs=[]))
                    break

        @blk.sync
        def _(sy):
            c.sd.inc(sy.dma_start(out=c.idsb[:, :], in_=c.ident), 16)
            c.sd.inc(sy.dma_start(out=c.onsb[:, :], in_=c.onesb), 16)
            c.sd.inc(sy.dma_start(out=c.hbsb[:, :], in_=c.hbase), 16)
            c.sd.inc(sy.dma_start(out=c.gtsb[:, :], in_=c.gate), 16)
            c.sd.inc(sy.dma_start(out=c.tbsb[:, :], in_=c.taub), 16)
            c.sd.inc(sy.dma_start(out=c.idfsb[:, :], in_=c.idf), 16)
            c.sd.inc(sy.dma_start(out=c.onfsb[:, :], in_=c.onesf), 16)
            c.sd.inc(sy.dma_start(out=c.epsb[:, :], in_=c.epsv), 16)
    c.consts_sd = c.sd.n


def emit_x16(c, X16, src, trows, half, xtmp, first_guard, nbuf=2):
    nc = c.nc
    ntc = trows // 128
    gdx = [c.gdx0, c.gdx1]
    sdx = [c.sdx0, c.sdx1]
    cast_done = {}
    tp0 = [sdx[0].n, sdx[1].n]
    if (blk := c.blk):
        @blk.gpsimd
        def _(g):
            for tc in range(ntc):
                b = tc % nbuf
                if tc >= nbuf:
                    g.wait_ge(sdx[b].h, tp0[b] + (tc // nbuf) * 16)
                elif tc == 0 and first_guard is not None:
                    for gd_ in first_guard:
                        g.wait_ge(*gd_)
                r0 = half * trows + tc * 128
                gdx[b].inc(g.dma_start(out=xtmp[:, b, :],
                                       in_=src[r0:r0 + 128, :]), 16)
                cast_done[tc] = gdx[b].n

        @blk.sync
        def _(sy):
            for tc in range(ntc):
                b = tc % nbuf
                sy.wait_ge(gdx[b].h, cast_done[tc])
                sdx[b].inc(sy.dma_start_transpose(
                    X16[:, :, tc * 128:(tc + 1) * 128],
                    xtmp[:, b, :].rearrange("p (c f) -> p c f", c=NF)), 16)


def emit_rope_mults(c, epair, opair, cos_sb, sin_sb, tok0, w, rtmp, par,
                    pe_guard, rtmp_guard=None):
    nc = c.nc
    if (blk := c.blk):
        @blk.vector
        def _(v):
            v.wait_ge(*pe_guard)
            v.wait_ge(*c.rope_tab_guard)
            if rtmp_guard is not None:
                v.wait_ge(*rtmp_guard)
            cs = cos_sb[:, tok0:tok0 + w]
            sn = sin_sb[:, tok0:tok0 + w]
            v.tensor_mul(rtmp[:, par, 0, 0:w], epair, cs)
            v.tensor_mul(rtmp[:, par, 1, 0:w], opair, sn)
            v.tensor_mul(rtmp[:, par, 2, 0:w], epair, sn)
            c.dv.inc(v.tensor_mul(rtmp[:, par, 3, 0:w], opair, cs), 1)
    return (c.dv.h, c.dv.n)


def emit_rope_combines(c, dst, chA, chB, tok0, w, rtmp, par, mul_guard):
    nc = c.nc
    if (blk := c.blk):
        @blk.gpsimd
        def _(p):
            p.wait_ge(*mul_guard)
            if c.rope_scr_guard is not None:
                p.wait_ge(*c.rope_scr_guard)
            p.tensor_sub(dst[0:64, chA, tok0:tok0 + w],
                         rtmp[0:64, par, 0, 0:w], rtmp[0:64, par, 1, 0:w])
            p.tensor_sub(c.scr[64:128, 0:w],
                         rtmp[64:128, par, 0, 0:w], rtmp[64:128, par, 1, 0:w])
            p.tensor_add(c.scr[0:64, 0:w],
                         rtmp[0:64, par, 2, 0:w], rtmp[0:64, par, 3, 0:w])
            c.po.inc(p.tensor_add(dst[64:128, chB, tok0:tok0 + w],
                                  rtmp[64:128, par, 2, 0:w],
                                  rtmp[64:128, par, 3, 0:w]), 1)
        @blk.sync
        def _(sy):
            sy.wait_ge(c.po.h, c.po.n)
            c.sdr.inc(sy.dma_start(out=dst[0:64, chB, tok0:tok0 + w],
                                   in_=c.scr[64:128, 0:w]), 16)
            c.sdr.inc(sy.dma_start(out=dst[64:128, chA, tok0:tok0 + w],
                                   in_=c.scr[0:64, 0:w]), 16)
            c.rope_scr_guard = (c.sdr.h, c.sdr.n)


def emit_wload(c, g, wbuf, buf, dram, r0, c0, ncols, fspan, nf0):
    src = dram[r0 + nf0 * 128: r0 + (nf0 + fspan) * 128, c0:c0 + ncols]
    src = src.rearrange("(a p) n -> p a n", p=128)
    sem = [c.gdw0, c.gdw1, c.gdw2, c.gdw3][buf]
    return sem.inc(g.dma_start(out=wbuf[:, buf, 0:fspan, 0:ncols], in_=src), 16)


def phase_kv(c):
    nc = c.nc
    FS = 4
    with ExitStack() as ph:
        X16h = ph.enter_context(nc.sbuf_tensor("X16h", [128, NF, 1024], BF16))
        xtmp = ph.enter_context(nc.sbuf_tensor("xtmp", [128, 2, D], BF16))
        wbuf = ph.enter_context(nc.sbuf_tensor("wbuf", [128, 2, 32, 256], BF16))
        cosf = ph.enter_context(nc.sbuf_tensor("cosf_sb", [128, T], F32))
        sinf = ph.enter_context(nc.sbuf_tensor("sinf_sb", [128, T], F32))
        rtmp = ph.enter_context(nc.sbuf_tensor("rtmp", [128, 2, 4, 1024], F32))
        psK = ph.enter_context(nc.psum_tensor("psK", [128, 2048], F32))
        psK2 = ph.enter_context(nc.psum_tensor("psK2", [128, 2048], F32))

        if (blk := c.blk):
            @blk.sync
            def _(sy):
                c.sd.inc(sy.dma_start(out=cosf[:, :], in_=c.cosf), 16)
                c.sd.inc(sy.dma_start(out=sinf[:, :], in_=c.sinf), 16)
        c.rope_tab_guard = (c.sd.h, c.sd.n)

        kps_guard = [None, None]
        phase_pe0 = c.pe.n
        unit = [0]
        pend_rope = []
        k_po0 = c.po.n
        for half in range(2):
            emit_x16(c, X16h, c.xb, 1024, half, xtmp,
                     [(c.pe.h, c.pe.n)] if (half or c.pe.n) else None)
            x16_ready = [(c.sdx0.h, c.sdx0.n), (c.sdx1.h, c.sdx1.n)]

            for r in range(4):
                po = (r % 2) * 1024
                pe0 = c.pe.n
                u0 = unit[0]
                if (blk := c.blk):
                    @blk.gpsimd
                    def _(g):
                        u = u0
                        if u >= 2:
                            g.wait_ge(c.pe.h, phase_pe0 + u - 1)
                        elif c.wbuf_guard is not None:
                            g.wait_ge(*c.wbuf_guard[u % 2])
                        emit_wload(c, g, wbuf, u % 2, c.wkv, 0, r * 256,
                                   256, 32, 0)
                        sem = [c.gdw0, c.gdw1][u % 2]
                        c.kw_done[0] = (sem.h, sem.n)

                    @blk.tensor
                    def _(t):
                        if r == 0:
                            for gx in x16_ready:
                                t.wait_ge(*gx)
                        u = u0
                        t.wait_ge(*c.kw_done[0])
                        if kps_guard[r % 2] is not None:
                            t.wait_ge(*kps_guard[r % 2])
                        for f in range(NF):
                            for n2 in range(2):
                                t.matmul(psK[:, po + n2 * 512:po + (n2 + 1) * 512],
                                         wbuf[:, u % 2, f, 0:128],
                                         X16h[:, f, n2 * 512:(n2 + 1) * 512],
                                         start=(f == 0), stop=(f == NF - 1))
                                mi = t.matmul(
                                    psK2[:, po + n2 * 512:po + (n2 + 1) * 512],
                                    wbuf[:, u % 2, f, 128:256],
                                    X16h[:, f, n2 * 512:(n2 + 1) * 512],
                                    start=(f == 0), stop=(f == NF - 1))
                        c.pe.inc(mi, 1)
                mg = emit_rope_mults(c, psK[:, po:po + 1024],
                                     psK2[:, po:po + 1024], cosf, sinf,
                                     half * 1024, 1024, rtmp, unit[0] % 2,
                                     (c.pe.h, pe0 + 1),
                                     (c.po.h, k_po0 + unit[0] - 1)
                                     if unit[0] >= 2 else None)
                kps_guard[r % 2] = mg
                pend_rope.append((2 * r, 2 * r + 1, half * 1024,
                                  unit[0] % 2, mg))
                unit[0] += 1
                if len(pend_rope) >= 2:
                    chA, chB, t0_, par, g_ = pend_rope.pop(0)
                    emit_rope_combines(c, c.K16, chA, chB, t0_, 1024, rtmp,
                                       par, g_)
        while pend_rope:
            chA, chB, t0_, par, g_ = pend_rope.pop(0)
            emit_rope_combines(c, c.K16, chA, chB, t0_, 1024, rtmp, par, g_)
        c.wbuf_guard = [(c.pe.h, c.pe.n)] * 4
        c.k_done_guards = [kps_guard[0], kps_guard[1],
                           (c.po.h, c.po.n), (c.sdr.h, c.sdr.n)]


def phase_v(c):
    nc = c.nc
    FS = 1
    with ExitStack() as ph:
        X16v = ph.enter_context(nc.sbuf_tensor("X16v", [128, NF, 1024], BF16))
        xtmp = ph.enter_context(nc.sbuf_tensor("xtmpv", [128, 1, D], BF16))
        wbuf = ph.enter_context(nc.sbuf_tensor("wbufv", [128, 4, FS, 512], BF16))
        psKv = ph.enter_context(nc.psum_tensor("psKv", [128, 2048], F32))
        psKv2 = ph.enter_context(nc.psum_tensor("psKv2", [128, 2048], F32))
        vps_guard = (c.dv.h, c.dv.n)
        nfb = NF // FS
        phase_pe0 = c.pe.n
        unit = [0]
        for half in range(2):
            emit_x16(c, X16v, c.xb, 1024, half, xtmp, [(c.pe.h, c.pe.n)],
                     nbuf=1)
            x16_ready = [(c.sdx0.h, c.sdx0.n), (c.sdx1.h, c.sdx1.n)]
            for vch in range(2):
                pe0 = c.pe.n
                u0 = unit[0]
                if (blk := c.blk):
                    @blk.gpsimd
                    def _(g):
                        for fb in range(nfb):
                            u = u0 + fb
                            if u >= 4:
                                g.wait_ge(c.pe.h, phase_pe0 + u - 3)
                            else:
                                g.wait_ge(*c.wbuf_guard[u % 4])
                            emit_wload(c, g, wbuf, u % 4, c.wkv, 0,
                                       1024 + vch * 512, 512, FS, fb * FS)
                            sem = [c.gdw0, c.gdw1, c.gdw2, c.gdw3][u % 4]
                            c.kw_done[fb] = (sem.h, sem.n)

                    @blk.tensor
                    def _(t):
                        if vch == 0:
                            for gx in x16_ready:
                                t.wait_ge(*gx)
                        for fb in range(nfb):
                            u = u0 + fb
                            t.wait_ge(*c.kw_done[fb])
                            if fb == 0 and vps_guard is not None:
                                t.wait_ge(*vps_guard)
                            for fi in range(FS):
                                f = fb * FS + fi
                                for tc4 in range(4):
                                    t.matmul(psKv[:, tc4 * 512:(tc4 + 1) * 512],
                                             X16v[:, f, tc4 * 128:(tc4 + 1) * 128],
                                             wbuf[:, u % 4, fi, 0:512],
                                             start=(f == 0), stop=(f == NF - 1))
                                    mi = t.matmul(
                                        psKv2[:, tc4 * 512:(tc4 + 1) * 512],
                                        X16v[:, f, (4 + tc4) * 128:(5 + tc4) * 128],
                                        wbuf[:, u % 4, fi, 0:512],
                                        start=(f == 0), stop=(f == NF - 1))
                            c.pe.inc(mi, 1)

                    @blk.scalar
                    def _(s):
                        s.wait_ge(c.pe.h, pe0 + nfb)
                        for tc4 in range(4):
                            s.copy(c.V16[:, half * 8 + tc4,
                                         vch * 512:(vch + 1) * 512],
                                   psKv[:, tc4 * 512:(tc4 + 1) * 512])
                            si = s.copy(c.V16[:, half * 8 + 4 + tc4,
                                              vch * 512:(vch + 1) * 512],
                                        psKv2[:, tc4 * 512:(tc4 + 1) * 512])
                        c.ac.inc(si, 1)
                vps_guard = (c.ac.h, c.ac.n)
                unit[0] += nfb
    c.wbuf_guard = [(c.pe.h, c.pe.n)] * 4
    c.v_done_guard = (c.ac.h, c.ac.n)


def phase_q(c):
    nc = c.nc
    FS = 8
    with ExitStack() as ph:
        X16q = ph.enter_context(nc.sbuf_tensor("X16q", [128, NF, TOWN], BF16))
        coso = ph.enter_context(nc.sbuf_tensor("coso_sb", [128, TOWN], F32))
        sino = ph.enter_context(nc.sbuf_tensor("sino_sb", [128, TOWN], F32))
        psA = ph.enter_context(nc.psum_tensor("psA", [128, 2048], F32))
        psB = ph.enter_context(nc.psum_tensor("psB", [128, 2048], F32))

        if (blk := c.blk):
            @blk.sync
            def _(sy):
                for gd_ in c.k_done_guards:
                    if gd_ is not None:
                        sy.wait_ge(*gd_)
                c.sd.inc(sy.dma_start(out=coso[:, :], in_=c.coso), 16)
                c.sd.inc(sy.dma_start(out=sino[:, :], in_=c.sino), 16)
        c.rope_tab_guard = (c.sd.h, c.sd.n)

        with ExitStack() as bst:
            xtmp = bst.enter_context(nc.sbuf_tensor("xtmpq", [128, 2, D], BF16))
            emit_x16(c, X16q, c.xq, TOWN, 0, xtmp,
                     [(c.pe.h, c.pe.n), (c.sdr.h, c.sdr.n),
                      (c.ac.h, c.ac.n), (c.dv.h, c.dv.n), (c.po.h, c.po.n)])
        x16_ready = [(c.sdx0.h, c.sdx0.n), (c.sdx1.h, c.sdx1.n)]
        rtmp = ph.enter_context(nc.sbuf_tensor("rtmpq", [128, 2, 4, 512], F32))
        wbuf = ph.enter_context(nc.sbuf_tensor("wbufq", [128, 2, 16, 256], BF16))

        qps_guard = [(c.dv.h, c.dv.n), (c.dv.h, c.dv.n)]
        phase_pe0 = c.pe.n
        pend_rope = []
        q_po0 = c.po.n
        for p in range(16):
            slot = p % 2
            po = slot * 1024
            pe0 = c.pe.n
            u0 = p * 2
            if (blk := c.blk):
                @blk.gpsimd
                def _(g):
                    for fh in range(2):
                        u = u0 + fh
                        if u >= 2:
                            g.wait_ge(c.pe.h, phase_pe0 + u - 1)
                        else:
                            g.wait_ge(*c.wbuf_guard[u % 2])
                        emit_wload(c, g, wbuf, u % 2, c.wq, 0, p * 256,
                                   256, 16, fh * 16)
                        sem = [c.gdw0, c.gdw1][u % 2]
                        c.kw_done[fh] = (sem.h, sem.n)

                @blk.tensor
                def _(t):
                    if p == 0:
                        for gx in x16_ready:
                            t.wait_ge(*gx)
                    for fh in range(2):
                        u = u0 + fh
                        t.wait_ge(*c.kw_done[fh])
                        if fh == 0 and qps_guard[slot] is not None:
                            t.wait_ge(*qps_guard[slot])
                        for fi in range(16):
                            f = fh * 16 + fi
                            for n2 in range(2):
                                t.matmul(psA[:, po + n2 * 512:po + (n2 + 1) * 512],
                                         wbuf[:, u % 2, fi, 0:128],
                                         X16q[:, f, n2 * 512:(n2 + 1) * 512],
                                         start=(f == 0), stop=(f == NF - 1))
                                mi = t.matmul(
                                    psB[:, po + n2 * 512:po + (n2 + 1) * 512],
                                    wbuf[:, u % 2, fi, 128:256],
                                    X16q[:, f, n2 * 512:(n2 + 1) * 512],
                                    start=(f == 0), stop=(f == NF - 1))
                        c.pe.inc(mi, 1)
            mg = None
            for th in range(2):
                par = (p * 2 + th) % 2
                u_ = p * 2 + th
                mg = emit_rope_mults(
                    c, psA[:, po + th * 512:po + (th + 1) * 512],
                    psB[:, po + th * 512:po + (th + 1) * 512],
                    coso, sino, th * 512, 512, rtmp, par,
                    (c.pe.h, pe0 + 2),
                    (c.po.h, q_po0 + u_ - 1) if u_ >= 2 else None)
                pend_rope.append((2 * p, 2 * p + 1, th * 512, par, mg))
                while len(pend_rope) >= 3:
                    chA, chB, t0_, par_, g_ = pend_rope.pop(0)
                    emit_rope_combines(c, c.Q16, chA, chB, t0_, 512, rtmp,
                                       par_, g_)
            qps_guard[slot] = mg
        while pend_rope:
            chA, chB, t0_, par_, g_ = pend_rope.pop(0)
            emit_rope_combines(c, c.Q16, chA, chB, t0_, 512, rtmp, par_, g_)
        c.wbuf_guard = [(c.pe.h, c.pe.n)] * 4


def emit_taps_p1(c):
    nc = c.nc
    c.tap_guard = None
    todo = [(n, t) for n, t in
            [("Q16", c.Q16), ("K16", c.K16), ("V16", c.V16)] if n in c.dbg]
    if not todo:
        return
    if (blk := c.blk):
        @blk.sync
        def _(sy):
            sy.wait_ge(c.sdr.h, c.sdr.n)
            sy.wait_ge(c.po.h, c.po.n)
            sy.wait_ge(c.ac.h, c.ac.n)
            for name, ten in todo:
                c.sd.inc(sy.dma_start(
                    out=c.dbg[name],
                    in_=ten[:, :, :].rearrange("p a b -> p (a b)")), 16)
    c.tap_guard = (c.sd.h, c.sd.n)


def phase_attn(c):
    nc = c.nc
    with ExitStack() as ph:
        masks = ph.enter_context(nc.sbuf_tensor("masks", [128, 16, 256], BF16))
        pt = ph.enter_context(nc.sbuf_tensor("pt", [128, 4, 4, 256], BF16))
        lntmp = ph.enter_context(nc.sbuf_tensor("lntmp", [128, 1024], F32))
        rrec = ph.enter_context(nc.sbuf_tensor("rrec", [128, 2, 1024], F32))
        scps = ph.enter_context(nc.psum_tensor("scps", [128, 2048], F32))
        psD = ph.enter_context(nc.psum_tensor("psD", [128, 1024], F32))
        psO = ph.enter_context(nc.psum_tensor("psO", [128, 1024], F32))

        if (blk := c.blk):
            @blk.gpsimd
            def _(g):
                g.wait_ge(c.pe.h, c.pe.n)
                c.gd.inc(g.dma_start(
                    out=masks[:, :, :],
                    in_=c.maskq.rearrange("p (a b) -> p a b", a=16)), 16)
        masks_ok = (c.gd.h, c.gd.n)

        work = []
        for h in range(NH):
            for s_ in range(4):
                for g in range(s_ + 1):
                    work.append((h, s_, g, g == s_))
        nwork = len(work)
        ex0 = c.ex.n
        ac0 = c.ac.n
        dv0 = c.dv.n
        sc_done = {}
        dp_done = {}
        dp_done_item = {}
        attn_sdr = c.sdr.n
        attn_po = c.po.n
        attn_ac = c.ac.n
        attn_dv = c.dv.n

        def recip_ac(h):
            return ac0 + 2 * h + 2

        def evac_dv(h):
            return dv0 + h + 1

        if (blk := c.blk):
            @blk.tensor
            def _(t):
                t.wait_ge(*masks_ok)
                t.wait_ge(c.sdr.h, attn_sdr)
                t.wait_ge(c.po.h, attn_po)
                t.wait_ge(c.ac.h, attn_ac)
                t.wait_ge(c.dv.h, attn_dv)
                pend = []

                def emit_dp(item):
                    wi2, (h, s_, g, last) = item
                    gkv = h >> 2
                    t.wait_ge(c.ex.h, ex0 + wi2 + 1)
                    if g == 0 and s_ == 0 and h >= 1:
                        t.wait_ge(c.ac.h, recip_ac(h - 1))
                        t.wait_ge(c.dv.h, evac_dv(h - 1))
                    so = s_ * 256
                    for j in range(4):
                        kb = 4 * g + j
                        st = (s_ % 2 == 0) and (kb == 0)
                        sp = last and (s_ % 2 == 1) and (j == 3)
                        t.matmul(psD[:, so:so + 256], c.onsb[:, :],
                                 pt[:, wi2 % 4, j, :], start=st, stop=sp,
                                 skip_group_check=True)
                        mi = t.matmul(psO[:, so:so + 256],
                                      c.V16[:, kb, gkv * 128:(gkv + 1) * 128],
                                      pt[:, wi2 % 4, j, :], start=st, stop=sp,
                                      skip_group_check=True)
                    c.pe.inc(mi, 1)
                    dp_done_item[wi2] = c.pe.n
                    if s_ == 3 and last:
                        dp_done[h] = c.pe.n

                for wi, (h, s_, g, last) in enumerate(work):
                    gkv = h >> 2
                    reg = (wi % 2) * 1024
                    if wi >= 2:
                        t.wait_ge(c.ex.h, ex0 + wi - 1)
                    for j in range(4):
                        kb = 4 * g + j
                        po2 = reg + j * 256
                        mi = t.matmul(scps[:, po2:po2 + 256],
                                      c.K16[:, gkv, kb * 128:(kb + 1) * 128],
                                      c.Q16[:, h, s_ * 256:(s_ + 1) * 256],
                                      start=True, stop=(not last),
                                      skip_group_check=True)
                        if last:
                            mi = t.matmul(scps[:, po2:po2 + 256],
                                          c.idsb[:, :],
                                          masks[:, s_ * 4 + j, :],
                                          start=False, stop=True,
                                          skip_group_check=True)
                    c.pe.inc(mi, 1)
                    sc_done[wi] = c.pe.n
                    pend.append((wi, (h, s_, g, last)))
                    if len(pend) >= 3:
                        emit_dp(pend.pop(0))
                while pend:
                    emit_dp(pend.pop(0))

            @blk.scalar
            def _(sc):
                for wi, (h, s_, g, last) in enumerate(work):
                    hp = h % 2
                    sc.wait_ge(c.pe.h, sc_done[wi])
                    if wi >= 4:
                        sc.wait_ge(c.pe.h, dp_done_item[wi - 4])
                    reg = (wi % 2) * 1024
                    a = sc.activation(
                        pt[:, wi % 4, :, :],
                        scps[:, reg:reg + 1024], AF.Exp, scale=SCALE)
                    c.ex.inc(a, 1)
                    if s_ == 3 and last:
                        sc.wait_ge(c.pe.h, dp_done[h])
                        c.ac.inc(sc.activation(lntmp[:, :], psD[:, :], AF.Ln), 1)
                        sc.wait_ge(c.ac.h, c.ac.n)
                        a2 = sc.activation(rrec[:, hp, :], lntmp[:, :],
                                           AF.Exp, scale=-1.0)
                        c.ac.inc(a2, 1)
                assert c.ac.n == ac0 + 2 * NH
                assert c.ex.n == ex0 + nwork

            @blk.vector
            def _(v):
                if c.tap_guard is not None:
                    v.wait_ge(*c.tap_guard)
                for h in range(NH):
                    hp = h % 2
                    v.wait_ge(c.ac.h, recip_ac(h))
                    c.dv.inc(v.tensor_mul(c.Q16[:, h, :], psO[:, :],
                                          rrec[:, hp, :]), 1)
                assert c.dv.n == dv0 + NH

        c.attn_done = {"pe": c.pe.n, "dv": c.dv.n, "ac": c.ac.n}

        d = c.dbg.get("AO")
        if d is not None:
            if (blk := c.blk):
                @blk.sync
                def _(sy):
                    sy.wait_ge(c.dv.h, c.dv.n)
                    c.sd.inc(sy.dma_start(
                        out=d, in_=c.Q16[:, :, :].rearrange("p a b -> p (a b)")), 16)


def phase_tail(c):
    nc = c.nc
    c.kvst.close()
    ad = c.attn_done
    with ExitStack() as ph:
        u = ph.enter_context(nc.sbuf_tensor("u", [128, 4, D], F32))
        wbr16 = ph.enter_context(nc.sbuf_tensor("wbr16", [128, 8, 1024], BF16))
        wlog16 = ph.enter_context(nc.sbuf_tensor("wlog16", [128, 32, 24], BF16))
        zsb = ph.enter_context(nc.sbuf_tensor("zsb", [128, 4, 24], F32))
        zrow = ph.enter_context(nc.sbuf_tensor("zrow", [24, 128], F32))
        usb = ph.enter_context(nc.sbuf_tensor("usb", [128, 4, 4], F32))
        vsb = ph.enter_context(nc.sbuf_tensor("vsb", [128, 4, 4], F32))
        tA = ph.enter_context(nc.sbuf_tensor("tA", [128, 4, 16], F32))
        tE = ph.enter_context(nc.sbuf_tensor("tE", [128, 4, 16], F32))
        ssb = ph.enter_context(nc.sbuf_tensor("ssb", [128, 4, 16], F32))
        csb = ph.enter_context(nc.sbuf_tensor("csb", [128, 4, 16], F32))
        hpre = ph.enter_context(nc.sbuf_tensor("hpre", [128, 4, 4], F32))
        hpost = ph.enter_context(nc.sbuf_tensor("hpost", [128, 4, 4], F32))
        srow = ph.enter_context(nc.sbuf_tensor("srow", [128, 4, 4], F32))
        sums = ph.enter_context(nc.sbuf_tensor("sums", [128, 16], F32))
        stat = ph.enter_context(nc.sbuf_tensor("stat", [128, 8], F32))
        gb = ph.enter_context(nc.sbuf_tensor("gb", [128, 1], F32))
        gb2 = ph.enter_context(nc.sbuf_tensor("gb2", [128, 1], F32))
        hb2 = ph.enter_context(nc.sbuf_tensor("hb2", [24, 1], F32))
        g1 = ph.enter_context(nc.sbuf_tensor("g1", [1, 4], F32))
        psW = ph.enter_context(nc.psum_tensor("psW", [128, 2048], F32))
        psY = ph.enter_context(nc.psum_tensor("psY", [128, 2048], F32))
        psL = psW[0:24, 1536:1664]
        psZt = psW[:, 1664:1688]

        if (blk := c.blk):
            @blk.gpsimd
            def _(g):
                g.wait_ge(c.pe.h, ad["pe"])
                g.wait_ge(c.dv.h, ad["dv"])
                c.gd.inc(g.dma_start(
                    out=wlog16[:, :, :],
                    in_=c.wlog.rearrange("(a p) n -> p a n", p=128)), 16)
                c.gd.inc(g.dma_start(
                    out=wbr16[:, :, :],
                    in_=c.wbr.rearrange("(a p) n -> p a n", p=128)), 16)

            @blk.vector
            def _(v):
                v.wait_ge(c.sd.h, c.sd.n)
                v.wait_ge(c.ac.h, ad["ac"])
                c.dv.inc(v.tensor_scalar_add(hb2[:, :], c.hbsb[:, :],
                                             -SINK_B / 10.0), 1)

            @blk.scalar
            def _(sc):
                sc.wait_ge(c.sd.h, c.sd.n)
                sc.wait_ge(c.dv.h, ad["dv"])
                sc.wait_ge(c.ac.h, ad["ac"])
                c.ac.inc(sc.activation(g1[:, 0:1], c.gtsb[:, :], AF.Exp,
                                       scale=-1.0), 1)
        if (blk := c.blk):
            @blk.vector
            def _(v):
                v.wait_ge(c.ac.h, c.ac.n)
                c.dv.inc(v.tensor_scalar_add(g1[:, 1:2], g1[:, 0:1], 1.0), 1)
            @blk.scalar
            def _(sc):
                sc.wait_ge(c.dv.h, c.dv.n)
                c.ac.inc(sc.activation(g1[:, 2:3], g1[:, 1:2], AF.Ln), 1)
                sc.wait_ge(c.ac.h, c.ac.n)
                c.ac.inc(sc.activation(g1[:, 3:4], g1[:, 2:3], AF.Exp,
                                       scale=-1.0), 1)
            @blk.tensor
            def _(t):
                t.wait_ge(c.ac.h, c.ac.n)
                c.pe.inc(t.matmul(psY[:, 0:1], c.onfsb[:, :], g1[:, 3:4],
                                  start=True, stop=True, skip_group_check=True), 1)
            @blk.vector
            def _(v):
                v.wait_ge(c.pe.h, c.pe.n)
                c.dv.inc(v.tensor_copy(gb[:, :], psY[:, 0:1]), 1)
                v.wait_ge(c.dv.h, c.dv.n)
                c.dv.inc(v.tensor_scalar(gb2[:, :], gb[:, :], -1.0, 2.0,
                                         OP.mult, OP.add), 1)
        tail_const_dv = c.dv.n
        tail_const_gd = c.gd.n

        xp16 = c.scr
        for sup in range(2):
            emit_super(c, sup, u, wbr16, wlog16,
                       xp16, zsb, zrow, usb, vsb, tA, tE, ssb,
                       csb, hpre, hpost, srow, sums, stat, gb, gb2, hb2,
                       psW, psL, psZt, psY, tail_const_dv, tail_const_gd)


def emit_super(c, sup, u, wbr16, wlog16, xp16,
               zsb, zrow, usb, vsb, tA, tE, ssb, csb, hpre,
               hpost, srow, sums, stat, gb, gb2, hb2, psW, psL, psZt, psY,
               tail_const_dv, tail_const_gd):
    nc = c.nc
    gdw = [c.gdw0, c.gdw1]
    sup_dv0 = c.dv.n
    sup_sdr0 = c.sdr.n
    sup_sd0 = c.sd.n
    stW = ExitStack()
    wob = stW.enter_context(nc.sbuf_tensor("wob_%d" % sup,
                                           [128, 2, 32, 256], BF16))

    wo_ld = {}
    pe0 = c.pe.n
    sup_po0 = c.po.n
    sup_ac0 = c.ac.n
    if (blk := c.blk):
        @blk.sync
        def _(sy):
            if sup == 0:
                sy.wait_ge(c.dv.h, tail_const_dv)
            else:
                sy.wait_ge(c.dv.h, sup_dv0)
                sy.wait_ge(c.ac.h, sup_ac0)
                sy.wait_ge(c.po.h, sup_po0)
            for tc in range(4):
                r0 = sup * 512 + tc * 128
                c.sdx0.inc(sy.dma_start(out=u[:, tc, :],
                                        in_=c.xq[r0:r0 + 128, :]), 16)
        upre_ok = (c.sdx0.h, c.sdx0.n)
    evac_dv0 = c.dv.n
    if (blk := c.blk):
        @blk.gpsimd
        def _(g):
            for g16 in range(16):
                if g16 < 2:
                    if sup == 0:
                        g.wait_ge(c.gd.h, tail_const_gd)
                        g.wait_ge(*c.wbuf_guard[g16 % 2])
                    else:
                        g.wait_ge(c.dv.h, sup_dv0)
                        g.wait_ge(c.sdp.h, c.sdp.n)
                        g.wait_ge(c.sd.h, sup_sd0)
                else:
                    g.wait_ge(c.pe.h, pe0 + (g16 - 1))
                gdw[g16 % 2].inc(g.dma_start(
                    out=wob[:, g16 % 2, :, :],
                    in_=c.wo[:, g16 * 256:(g16 + 1) * 256]
                    .rearrange("(a p) n -> p a n", p=128)), 16)
                wo_ld[g16] = gdw[g16 % 2].n

        @blk.tensor
        def _(t):
            if sup:
                t.wait_ge(c.dv.h, sup_dv0)
            else:
                t.wait_ge(c.dv.h, c.attn_done["dv"])
                t.wait_ge(c.ac.h, c.attn_done["ac"])
            for g16 in range(16):
                t.wait_ge(gdw[g16 % 2].h, wo_ld[g16])
                if g16 >= 2:
                    t.wait_ge(c.dv.h, evac_dv0 + (g16 - 1) * 4)
                reg = (g16 % 2) * 1024
                for d in range(32):
                    for tc in range(4):
                        mi = t.matmul(
                            psW[:, reg + tc * 256:reg + (tc + 1) * 256],
                            c.Q16[:, d, sup * 512 + tc * 128:
                                  sup * 512 + (tc + 1) * 128],
                            wob[:, g16 % 2, d, :],
                            start=(d == 0 and tc % 2 == 0), stop=(d == 31),
                            skip_group_check=True)
                c.pe.inc(mi, 1)

        @blk.vector
        def _(v):
            v.wait_ge(*upre_ok)
            assert c.dv.n == evac_dv0
            for g16 in range(16):
                v.wait_ge(c.pe.h, pe0 + g16 + 1)
                reg = (g16 % 2) * 1024
                for tc in range(4):
                    a = v.tensor_add(u[:, tc, g16 * 256:(g16 + 1) * 256],
                                     psW[:, reg + tc * 256:reg + (tc + 1) * 256],
                                     u[:, tc, g16 * 256:(g16 + 1) * 256])
                    c.dv.inc(a, 1)
    wo_pe = c.pe.n
    wo_evac_dv = c.dv.n

    stW.close()
    st4 = ExitStack()
    ptmp = st4.enter_context(nc.sbuf_tensor("ptmp_%d" % sup, [128, 1024], F32, side="right"))
    xp = st4.enter_context(nc.sbuf_tensor("xp_%d" % sup, [128, 1024], F32, side="right"))
    xpT = st4.enter_context(nc.sbuf_tensor("xpT_%d" % sup, [128, 8, 128], BF16, side="right"))
    st2 = ExitStack()
    xn16 = st2.enter_context(nc.sbuf_tensor("xn16_%d" % sup, [128, 2, D], BF16))
    xnT = st2.enter_context(nc.sbuf_tensor("xnT_%d" % sup, [128, 2, 32, 128], BF16))

    radd_dv = {}
    if (blk := c.blk):
        @blk.vector
        def _(v):
            v.wait_ge(c.dv.h, wo_evac_dv)
            for tc in range(4):
                c.dv.inc(v.reduce_sum(stat[:, tc:tc + 1].unsqueeze(1),
                                      u[:, tc, :].unsqueeze(1), AX.X), 1)
                radd_dv[tc] = c.dv.n

    if (blk := c.blk):
        @blk.scalar
        def _(sc):
            for tc in range(4):
                if tc:
                    sc.wait_ge(c.ac.h, c.ac.n)
                sc.wait_ge(c.dv.h, radd_dv[tc])
                c.ac.inc(sc.activation(xn16[:, tc % 2, :], u[:, tc, :],
                                       AF.Square,
                                       accum_out=stat[:, 4 + tc:5 + tc]), 1)
        sq_ac = c.ac.n

        @blk.vector
        def _(v):
            v.wait_ge(c.ac.h, sq_ac)
            c.dv.inc(v.tensor_scalar_mul(stat[:, 0:4], stat[:, 0:4], 1.0 / D), 1)
            v.wait_ge(c.dv.h, c.dv.n)
            c.dv.inc(v.tensor_mul(sums[:, 0:4], stat[:, 0:4], stat[:, 0:4]), 1)
            v.wait_ge(c.dv.h, c.dv.n)
            c.dv.inc(v.scalar_tensor_tensor(sums[:, 4:8], stat[:, 4:8],
                                            1.0 / D, sums[:, 0:4],
                                            OP.mult, OP.subtract), 1)
        var_dv = c.dv.n

        @blk.scalar
        def _(sc):
            sc.wait_ge(c.dv.h, var_dv)
            c.ac.inc(sc.activation(sums[:, 4:8], sums[:, 4:8], AF.Ln,
                                   bias=c.epsb[:, 0:1]), 1)
            sc.wait_ge(c.ac.h, c.ac.n)
            c.ac.inc(sc.activation(sums[:, 0:4], sums[:, 4:8], AF.Exp,
                                   scale=-0.5), 1)
        rstd_ac = c.ac.n

        @blk.vector
        def _(v):
            v.wait_ge(c.ac.h, rstd_ac)
            for tc in range(4):
                c.dv.inc(v.tensor_scalar(u[:, tc, :], u[:, tc, :],
                                         stat[:, tc:tc + 1],
                                         sums[:, tc:tc + 1],
                                         OP.subtract, OP.mult), 1)
                radd_dv[tc] = c.dv.n

        @blk.scalar
        def _(sc):
            for tc in range(4):
                if tc:
                    sc.wait_ge(c.ac.h, c.ac.n)
                sc.wait_ge(c.dv.h, radd_dv[tc])
                c.ac.inc(sc.activation(xn16[:, tc % 2, :], u[:, tc, :],
                                       AF.Square,
                                       accum_out=stat[:, 4 + tc:5 + tc]), 1)
        rsq_ac = c.ac.n

        @blk.scalar
        def _(sc):
            sc.wait_ge(c.ac.h, rsq_ac)
            c.ac.inc(sc.activation(sums[:, 4:8], stat[:, 4:8], AF.Ln,
                                   bias=c.epsb[:, 1:2], scale=1.0 / D), 1)
            sc.wait_ge(c.ac.h, c.ac.n)
            c.ac.inc(sc.activation(sums[:, 8:12], sums[:, 4:8], AF.Exp,
                                   scale=-0.5), 1)
        rms_ac = c.ac.n

        @blk.vector
        def _(v):
            v.wait_ge(c.ac.h, rms_ac)
            sdt_s6 = [c.sdt.n, c.sdt1.n]
            for tc in range(4):
                if tc >= 2:
                    v.wait_ge([c.sdt, c.sdt1][tc % 2].h, sdt_s6[tc % 2] + 16)
                c.dv.inc(v.tensor_scalar_mul(xn16[:, tc % 2, :], u[:, tc, :],
                                             sums[:, 8 + tc:9 + tc]), 1)
                c.xn_dv[tc] = c.dv.n

    for tc in range(4):
        if (blk := c.blk):
            @blk.sync
            def _(sy):
                sy.wait_ge(c.dv.h, c.xn_dv[tc])
                prev = c.logit_mm.get(tc - 2 if tc >= 2 else "last")
                if prev:
                    sy.wait_ge(c.pe.h, prev)
                [c.sdt, c.sdt1][tc % 2].inc(sy.dma_start_transpose(
                    xnT[:, tc % 2, :, :],
                    xn16[:, tc % 2, :].rearrange("p (a b) -> p a b", a=32)), 16)
            xnT_ok = ([c.sdt, c.sdt1][tc % 2].h, [c.sdt, c.sdt1][tc % 2].n)

            @blk.tensor
            def _(t):
                t.wait_ge(*xnT_ok)
                if c.zrow_used:
                    t.wait_ge(c.dv.h, c.zrow_used)
                for f in range(32):
                    mi = t.matmul(psL, wlog16[:, f, :],
                                  xnT[:, tc % 2, f, :],
                                  start=(f == 0), stop=(f == 31),
                                  skip_group_check=True)
                c.pe.inc(mi, 1)
                c.logit_mm[tc] = c.pe.n
                c.logit_mm["last"] = c.pe.n

            @blk.vector
            def _(v):
                v.wait_ge(c.pe.h, c.logit_mm[tc])
                c.dv.inc(v.tensor_scalar(zrow[:, :], psL,
                                         hb2[:, :], c.tbsb[:, :],
                                         OP.add, OP.mult), 1)
                c.zrow_used = c.dv.n

            @blk.tensor
            def _(t):
                t.wait_ge(c.dv.h, c.zrow_used)
                if c.zsb_used:
                    t.wait_ge(c.dv.h, c.zsb_used)
                c.pe.inc(t.transpose(psZt, zrow[:, :],
                                     c.idfsb[0:24, 0:24]), 1)
                zt_pe = c.pe.n

            @blk.vector
            def _(v):
                v.wait_ge(c.pe.h, c.pe.n)
                c.dv.inc(v.tensor_copy(zsb[:, tc, :], psZt), 1)
                c.zsb_used = c.dv.n
    zall_dv = c.dv.n
    st2.close()

    if (blk := c.blk):
        for off, dst in ((16, hpre), (20, hpost)):
            @blk.scalar
            def _(sc, off=off, dst=dst):
                sc.wait_ge(c.dv.h, c.dv.n)
                sc.wait_ge(c.ac.h, c.ac.n)
                c.ac.inc(sc.activation(srow[:, :, :], zsb[:, :, off:off + 4],
                                       AF.Exp), 1)

            @blk.vector
            def _(v, off=off, dst=dst):
                v.wait_ge(c.ac.h, c.ac.n)
                c.dv.inc(v.reduce_sum(stat[:, 0:4].rearrange("p (c o) -> p c o", c=4),
                                      srow[:, :, :], AX.X), 1)

            @blk.scalar
            def _(sc, off=off, dst=dst):
                sc.wait_ge(c.dv.h, c.dv.n)
                c.ac.inc(sc.activation(stat[:, 4:8], stat[:, 0:4], AF.Ln), 1)
                sc.wait_ge(c.ac.h, c.ac.n)
                c.ac.inc(sc.activation(stat[:, 0:4], stat[:, 4:8], AF.Exp,
                                       scale=-1.0), 1)

            @blk.vector
            def _(v, off=off, dst=dst):
                v.wait_ge(c.ac.h, c.ac.n)
                c.dv.inc(v.tensor_mul(
                    dst[:, :, :], srow[:, :, :],
                    stat[:, 0:4].rearrange("p (c o) -> p c o", c=4)
                    .broadcast_to([128, 4, 4])), 1)
    h_dv = c.dv.n

    yreg = [psW[:, 0:1024], psW[:, 1024:2048], psY[:, 0:1024], psY[:, 1024:2048]]
    y_pe = {}
    for tc in range(4):
        if (blk := c.blk):
            @blk.gpsimd
            def _(p):
                p.wait_ge(c.dv.h, h_dv)
                if c.xp_po_used is not None:
                    p.wait_ge(c.sdp.h, c.xp_po_used)
                x1c = u[:, tc, :].rearrange("p (n k) -> p n k", n=4)
                c.po.inc(p.tensor_mul(
                    xp[:, :], x1c[:, 0, :],
                    hpre[:, tc, 0:1].broadcast_to([128, 1024])), 1)
                for n in range(1, 4):
                    p.wait_ge(c.po.h, c.po.n)
                    c.po.inc(p.tensor_mul(
                        ptmp[:, :], x1c[:, n, :],
                        hpre[:, tc, n:n + 1].broadcast_to([128, 1024])), 1)
                    p.wait_ge(c.po.h, c.po.n)
                    c.po.inc(p.tensor_add(xp[:, :], xp[:, :], ptmp[:, :]), 1)
                p.wait_ge(c.po.h, c.po.n)
                c.po.inc(p.tensor_copy(xp16[:, :], xp[:, :]), 1)
            xp_po = c.po.n

            @blk.sync
            def _(sy):
                sy.wait_ge(c.po.h, xp_po)
                if c.y_mm:
                    sy.wait_ge(c.pe.h, c.y_mm)
                c.sdp.inc(sy.dma_start_transpose(
                    xpT[:, :, :],
                    xp16[:, :].rearrange("p (a b) -> p a b", a=8)), 16)
                c.xp_po_used = c.sdp.n
            xpT_ok = c.sdp.n

            @blk.tensor
            def _(t):
                t.wait_ge(c.sdp.h, xpT_ok)
                for co in range(2):
                    for ci in range(8):
                        mi = t.matmul(yreg[tc][:, co * 512:(co + 1) * 512],
                                      xpT[:, ci, :],
                                      wbr16[:, ci, co * 512:(co + 1) * 512],
                                      start=(ci == 0), stop=(ci == 7),
                                      skip_group_check=True)
                    c.pe.inc(mi, 1)
                c.y_mm = c.pe.n
                y_pe[tc] = c.pe.n

    if (blk := c.blk):
        zr = zsb[:, :, 0:16].rearrange("p c (i j) -> p c i j", i=4)
        zrT = zr.transpose([0, 1, 3, 2])
        sk0_dv = c.dv.n

        for it in range(50):
            @blk.vector
            def _(v, it=it):
                if it == 0:
                    v.wait_ge(c.dv.h, sk0_dv)
                else:
                    v.wait_ge(c.ac.h, c.ac.n)
                if it == 0:
                    c.dv.inc(v.tensor_copy(
                        tA[:, :, :].rearrange("p c (i j) -> p c i j", i=4), zr), 1)
                else:
                    c.dv.inc(v.tensor_sub(
                        tA[:, :, :].rearrange("p c (i j) -> p c i j", i=4),
                        zr, vsb[:, :, :].unsqueeze(2).broadcast_to(
                            [128, 4, 4, 4])), 1)

            @blk.scalar
            def _(sc, it=it):
                sc.wait_ge(c.dv.h, c.dv.n)
                c.ac.inc(sc.activation(tE[:, :, :], tA[:, :, :], AF.Exp), 1)

            @blk.vector
            def _(v, it=it):
                v.wait_ge(c.ac.h, c.ac.n)
                c.dv.inc(v.reduce_sum(
                    usb[:, :, :],
                    tE[:, :, :].rearrange("p c (i j) -> p c i j", i=4), AX.X), 1)

            @blk.scalar
            def _(sc, it=it):
                sc.wait_ge(c.dv.h, c.dv.n)
                c.ac.inc(sc.activation(usb[:, :, :], usb[:, :, :], AF.Ln), 1)

            @blk.vector
            def _(v, it=it):
                v.wait_ge(c.ac.h, c.ac.n)
                c.dv.inc(v.tensor_sub(
                    tA[:, :, :].rearrange("p c (j i) -> p c j i", j=4),
                    zrT, usb[:, :, :].unsqueeze(2).broadcast_to(
                        [128, 4, 4, 4])), 1)

            @blk.scalar
            def _(sc, it=it):
                sc.wait_ge(c.dv.h, c.dv.n)
                c.ac.inc(sc.activation(tE[:, :, :], tA[:, :, :], AF.Exp), 1)

            @blk.vector
            def _(v, it=it):
                v.wait_ge(c.ac.h, c.ac.n)
                c.dv.inc(v.reduce_sum(
                    vsb[:, :, :],
                    tE[:, :, :].rearrange("p c (j i) -> p c j i", j=4), AX.X), 1)

            @blk.scalar
            def _(sc, it=it):
                sc.wait_ge(c.dv.h, c.dv.n)
                c.ac.inc(sc.activation(vsb[:, :, :], vsb[:, :, :], AF.Ln), 1)

        @blk.vector
        def _(v):
            v.wait_ge(c.ac.h, c.ac.n)
            c.dv.inc(v.tensor_sub(
                tA[:, :, :].rearrange("p c (i j) -> p c i j", i=4),
                zr, usb[:, :, :].unsqueeze(3).broadcast_to([128, 4, 4, 4])), 1)
            v.wait_ge(c.dv.h, c.dv.n)
            c.dv.inc(v.tensor_sub(
                tA[:, :, :].rearrange("p c (i j) -> p c i j", i=4),
                tA[:, :, :].rearrange("p c (i j) -> p c i j", i=4),
                vsb[:, :, :].unsqueeze(2).broadcast_to([128, 4, 4, 4])), 1)

        @blk.scalar
        def _(sc):
            sc.wait_ge(c.dv.h, c.dv.n)
            c.ac.inc(sc.activation(ssb[:, :, :], tA[:, :, :], AF.Exp), 1)

        @blk.vector
        def _(v):
            v.wait_ge(c.ac.h, c.ac.n)
            c.dv.inc(v.tensor_scalar_mul(csb[:, :, :], ssb[:, :, :],
                                         gb[:, 0:1]), 1)
            v.wait_ge(c.dv.h, c.dv.n)
            c.dv.inc(v.tensor_scalar_add(
                csb[:, :, 0:16:5].rearrange("p c i -> p c i"),
                csb[:, :, 0:16:5].rearrange("p c i -> p c i"),
                gb2[:, 0:1]), 1)
    sink_dv = c.dv.n

    st3 = ExitStack()
    mixt = st3.enter_context(nc.sbuf_tensor("mixt_%d" % sup, [128, 4, 1024], F32))
    for tc in range(4):
        if (blk := c.blk):
            def mix_x(eng, sem, i, tc=tc, pool=False):
                x1c = u[:, tc, :].rearrange("p (n k) -> p n k", n=4)
                if pool:
                    sem.inc(eng.tensor_mul(
                        mixt[:, i, :], x1c[:, 0, :],
                        csb[:, tc, 4 * i:4 * i + 1].broadcast_to([128, 1024])), 1)
                    for n in range(1, 4):
                        eng.wait_ge(sem.h, sem.n)
                        sem.inc(eng.tensor_mul(
                            ptmp[:, :], x1c[:, n, :],
                            csb[:, tc, 4 * i + n:4 * i + n + 1]
                            .broadcast_to([128, 1024])), 1)
                        eng.wait_ge(sem.h, sem.n)
                        sem.inc(eng.tensor_add(mixt[:, i, :], mixt[:, i, :],
                                               ptmp[:, :]), 1)
                    return
                sem.inc(eng.tensor_scalar_mul(mixt[:, i, :], x1c[:, 0, :],
                                              csb[:, tc, 4 * i:4 * i + 1]), 1)
                for n in range(1, 4):
                    eng.wait_ge(sem.h, sem.n)
                    sem.inc(eng.scalar_tensor_tensor(
                        mixt[:, i, :], x1c[:, n, :],
                        csb[:, tc, 4 * i + n:4 * i + n + 1],
                        mixt[:, i, :], OP.mult, OP.add), 1)

            def mix_y(v, i, tc=tc):
                v.wait_ge(c.dv.h, c.dv.n)
                c.dv.inc(v.scalar_tensor_tensor(
                    mixt[:, i, :], yreg[tc][:, :],
                    hpost[:, tc, i:i + 1],
                    mixt[:, i, :], OP.mult, OP.add,
                    accum_out=sums[:, i:i + 1]), 1)

            @blk.gpsimd
            def _(p):
                p.wait_ge(c.dv.h, sink_dv)
                if c.last_out_sd is not None:
                    p.wait_ge(c.sd.h, c.last_out_sd)
                mix_x(p, c.po, 0, pool=True)
            mixp_po = c.po.n

            @blk.vector
            def _(v):
                v.wait_ge(c.dv.h, sink_dv)
                v.wait_ge(c.pe.h, y_pe[tc])
                if c.last_out_sd is not None:
                    v.wait_ge(c.sd.h, c.last_out_sd)
                mix_x(v, c.dv, 1)
                mix_y(v, 1)
                mix_x(v, c.dv, 2)
                mix_y(v, 2)
                mix_x(v, c.dv, 3)
                mix_y(v, 3)
                v.wait_ge(c.po.h, mixp_po)
                mix_y(v, 0)
                v.wait_ge(c.dv.h, c.dv.n)
                c.dv.inc(v.reduce_sum(stat[:, 0:1].unsqueeze(1),
                                      sums[:, 0:4].unsqueeze(1), AX.X), 1)
            mix_dv = c.dv.n

            @blk.scalar
            def _(sc):
                sc.wait_ge(c.dv.h, mix_dv)
                c.ac.inc(sc.activation(u[:, tc, :],
                                       mixt[:, :, :].rearrange("p a b -> p (a b)"),
                                       AF.Square,
                                       accum_out=stat[:, 1:2]), 1)
            lnsq_ac = c.ac.n

            @blk.vector
            def _(v):
                v.wait_ge(c.ac.h, lnsq_ac)
                c.dv.inc(v.tensor_scalar_mul(stat[:, 2:3], stat[:, 0:1],
                                             1.0 / D), 1)
                v.wait_ge(c.dv.h, c.dv.n)
                c.dv.inc(v.tensor_mul(stat[:, 3:4], stat[:, 2:3],
                                      stat[:, 2:3]), 1)
                v.wait_ge(c.dv.h, c.dv.n)
                c.dv.inc(v.scalar_tensor_tensor(stat[:, 4:5], stat[:, 1:2],
                                                1.0 / D, stat[:, 3:4],
                                                OP.mult, OP.subtract), 1)

            @blk.scalar
            def _(sc):
                sc.wait_ge(c.dv.h, c.dv.n)
                c.ac.inc(sc.activation(stat[:, 5:6], stat[:, 4:5], AF.Ln,
                                       bias=c.epsb[:, 0:1]), 1)
                sc.wait_ge(c.ac.h, c.ac.n)
                c.ac.inc(sc.activation(stat[:, 6:7], stat[:, 5:6], AF.Exp,
                                       scale=-0.5), 1)

            @blk.vector
            def _(v):
                v.wait_ge(c.ac.h, c.ac.n)
                for i in range(4):
                    a = v.tensor_scalar(mixt[:, i, :], mixt[:, i, :],
                                        stat[:, 2:3], stat[:, 6:7],
                                        OP.subtract, OP.mult)
                c.dv.inc(a, 1)
            out_dv = c.dv.n

            @blk.sync
            def _(sy):
                sy.wait_ge(c.dv.h, out_dv)
                r0 = sup * 512 + tc * 128
                c.sd.inc(sy.dma_start(
                    out=c.out[r0:r0 + 128, :],
                    in_=mixt[:, :, :].rearrange("p a b -> p (a b)")), 16)
                c.last_out_sd = c.sd.n
    st2.close()
    st3.close()
    st4.close()
    if sup == 1:
        if (blk := c.blk):
            @blk.sync
            def _(sy):
                sy.wait_ge(c.sd.h, c.sd.n)



def rope_perm_cols(nheads):
    idx = []
    for p in range(nheads // 2):
        for eo in range(2):
            for a in range(2):
                h = 2 * p + a
                for i in range(64):
                    idx.append(h * 128 + 2 * i + eo)
    return np.array(idx)


def prep_core_inputs(inputs, core):
    b, ch = core >> 1, core & 1
    J = JSETS[ch]
    x = np.asarray(inputs["L0_attn_x"])
    w_qkv = np.asarray(inputs["L0_attn_w_qkv"])
    w_o = np.asarray(inputs["L0_attn_w_o"])

    own_rows = np.concatenate([np.arange(j * 256, (j + 1) * 256) for j in J])
    xb = np.ascontiguousarray(x[b])
    xq = np.ascontiguousarray(x[b][own_rows])

    wq = np.ascontiguousarray(w_qkv[:, rope_perm_cols(NH)])
    kperm = 4096 + rope_perm_cols(NKV)
    wkv = np.ascontiguousarray(
        np.concatenate([w_qkv[:, kperm], w_qkv[:, 5120:6144]], axis=1))

    inv_freq = 1.0 / np.power(np.float64(10000.0),
                               np.arange(0, HD, 2, dtype=np.float64) / HD)
    angles = np.arange(T, dtype=np.float64)[:, None] * inv_freq[None, :]
    cos_t = np.cos(angles).astype(np.float32).T
    sin_t = np.sin(angles).astype(np.float32).T
    cosf = np.ascontiguousarray(np.vstack([cos_t, cos_t]))
    sinf = np.ascontiguousarray(np.vstack([sin_t, sin_t]))
    coso = np.ascontiguousarray(cosf[:, own_rows])
    sino = np.ascontiguousarray(sinf[:, own_rows])

    maskq = np.zeros((128, 4096), np.float32)
    for k, j in enumerate(J):
        nat = 2 * j + 2
        w0 = 4 * (k + 1) - 4
        qg = np.arange(j * 256, (j + 1) * 256)
        for wi in range(4):
            kb = w0 + wi
            kg = np.arange(kb * 128, (kb + 1) * 128)
            m = np.where(kg[:, None] <= qg[None, :], 0.0, NEG).astype(np.float32)
            maskq[:, (k * 4 + wi) * 256:(k * 4 + wi + 1) * 256] = m

    hbase = np.concatenate([
        np.asarray(inputs["L0_ffn_H_res_base"]).reshape(16),
        np.asarray(inputs["L0_ffn_H_pre_base"]).reshape(4),
        np.asarray(inputs["L0_ffn_H_post_base"]).reshape(4)]).astype(
            np.float32).reshape(24, 1)
    wlog = np.ascontiguousarray(np.concatenate(
        [np.asarray(inputs["L0_ffn_W_res"]), np.asarray(inputs["L0_ffn_W_pre"]),
         np.asarray(inputs["L0_ffn_W_post"])], axis=0).T.astype(np.float32))

    return {
        "xb": xb, "xq": xq, "wq": wq, "wkv": wkv,
        "wo": np.ascontiguousarray(w_o),
        "wbr": np.ascontiguousarray(np.asarray(inputs["L0_ffn_W_branch"])),
        "wlog": wlog, "cosf": cosf, "sinf": sinf, "coso": coso, "sino": sino,
        "maskq": maskq,
        "ident": np.eye(128, dtype=BF),
        "onesb": np.ones((128, 128), dtype=BF),
        "hbase": hbase,
        "gate": np.asarray(inputs["L0_ffn_gate"]).reshape(1, 1).astype(np.float32),
        "taub": np.concatenate([np.full(16, 10.0, np.float32),
                                np.ones(8, np.float32)]).reshape(24, 1),
        "idf": np.eye(128, dtype=np.float32),
        "onesf": np.ones((1, 128), np.float32),
        "epsv": np.tile(np.array([[LN_EPS, RMS_EPS]], np.float32), (128, 1)),
    }


def assemble_output(results):
    out = np.zeros((4, T, D), np.float32)
    for core, r in enumerate(results):
        b, ch = core >> 1, core & 1
        o = r["out"]
        for k, j in enumerate(JSETS[ch]):
            out[b, j * 256:(j + 1) * 256] = o[k * 256:(k + 1) * 256]
    return out




from concourse.bass_utils import run_bass_kernel_spmd


def kernel(L0_attn_x, L0_attn_w_qkv, L0_attn_w_o, L0_ffn_W_res, L0_ffn_W_pre,
           L0_ffn_W_post, L0_ffn_H_res_base, L0_ffn_H_pre_base,
           L0_ffn_H_post_base, L0_ffn_gate, L0_ffn_W_branch):
    inputs = {
        "L0_attn_x": L0_attn_x, "L0_attn_w_qkv": L0_attn_w_qkv,
        "L0_attn_w_o": L0_attn_w_o, "L0_ffn_W_res": L0_ffn_W_res,
        "L0_ffn_W_pre": L0_ffn_W_pre, "L0_ffn_W_post": L0_ffn_W_post,
        "L0_ffn_H_res_base": L0_ffn_H_res_base,
        "L0_ffn_H_pre_base": L0_ffn_H_pre_base,
        "L0_ffn_H_post_base": L0_ffn_H_post_base,
        "L0_ffn_gate": L0_ffn_gate, "L0_ffn_W_branch": L0_ffn_W_branch,
    }
    nc, c = build()
    nc.finalize()
    in_maps = [prep_core_inputs(inputs, core) for core in range(8)]
    res = run_bass_kernel_spmd(nc, in_maps, core_ids=list(range(8)))
    return assemble_output(res.results)


# version 12 (final)
# speedup vs baseline: 1.1399x; 1.1399x over previous
import numpy as np
import ml_dtypes
from contextlib import ExitStack
import concourse.bass as bass
import concourse.mybir as mybir

F32 = mybir.dt.float32
BF16 = mybir.dt.bfloat16
AF = mybir.ActivationFunctionType
AX = mybir.AxisListType
OP = mybir.AluOpType
BF = ml_dtypes.bfloat16

D = 4096
T = 2048
TOWN = 1024
NH = 32
NKV = 8
HD = 128
NF = D // 128
SCALE = float(1.0 / np.sqrt(np.float32(HD)))
RMS_EPS = 1.1920929e-07
LN_EPS = 1e-5
SINK_B = 24.0
NEG = -1e30
JSETS = [[0, 2, 5, 7], [1, 3, 4, 6]]


class Sem:
    def __init__(self, stack, nc, name):
        self.h = stack.enter_context(nc.semaphore(name))
        self.n = 0

    def inc(self, instr, k=1):
        instr.then_inc(self.h, k)
        self.n += k
        return self.n


class Ctx:
    pass


def build(debug_taps=(), upto="all"):
    nc = bass.Bass("TRN2", target_bir_lowering=False, debug=False)
    c = Ctx()
    c.nc = nc
    c.taps = set(debug_taps)

    c.xb = nc.dram_tensor("xb", [T, D], F32, kind="ExternalInput").ap()
    c.xq = nc.dram_tensor("xq", [TOWN, D], F32, kind="ExternalInput").ap()
    c.wq = nc.dram_tensor("wq", [D, D], F32, kind="ExternalInput").ap()
    c.wkv = nc.dram_tensor("wkv", [D, 2048], F32, kind="ExternalInput").ap()
    c.wo = nc.dram_tensor("wo", [D, D], F32, kind="ExternalInput").ap()
    c.wbr = nc.dram_tensor("wbr", [1024, 1024], F32, kind="ExternalInput").ap()
    c.wlog = nc.dram_tensor("wlog", [D, 24], F32, kind="ExternalInput").ap()
    c.cosf = nc.dram_tensor("cosf", [128, T], F32, kind="ExternalInput").ap()
    c.sinf = nc.dram_tensor("sinf", [128, T], F32, kind="ExternalInput").ap()
    c.coso = nc.dram_tensor("coso", [128, TOWN], F32, kind="ExternalInput").ap()
    c.sino = nc.dram_tensor("sino", [128, TOWN], F32, kind="ExternalInput").ap()
    c.maskq = nc.dram_tensor("maskq", [128, 4096], F32, kind="ExternalInput").ap()
    c.ident = nc.dram_tensor("ident", [128, 128], BF16, kind="ExternalInput").ap()
    c.onesb = nc.dram_tensor("onesb", [128, 128], BF16, kind="ExternalInput").ap()
    c.hbase = nc.dram_tensor("hbase", [24, 1], F32, kind="ExternalInput").ap()
    c.gate = nc.dram_tensor("gate", [1, 1], F32, kind="ExternalInput").ap()
    c.taub = nc.dram_tensor("taub", [24, 1], F32, kind="ExternalInput").ap()
    c.idf = nc.dram_tensor("idf", [24, 24], F32, kind="ExternalInput").ap()
    c.onesf = nc.dram_tensor("onesf", [1, 128], F32, kind="ExternalInput").ap()
    c.epsv = nc.dram_tensor("epsv", [128, 2], F32, kind="ExternalInput").ap()
    c.out = nc.dram_tensor("out", [TOWN, D], F32, kind="ExternalOutput").ap()
    c.dbg = {}
    for name, shape in [("Q16", [128, NH * TOWN]), ("K16", [128, NKV * T]),
                        ("V16", [128, 16 * 1024]), ("AO", [128, NH * TOWN]),
                        ("U", [TOWN, D]), ("X1", [TOWN, D]),
                        ("LOG", [TOWN, 32]), ("SNK", [128, 8 * 32]),
                        ("Y", [TOWN, 1024])]:
        if name in c.taps:
            dt = BF16 if name in ("Q16", "K16", "V16", "AO") else F32
            c.dbg[name] = nc.dram_tensor("dbg_" + name, shape, dt,
                                         kind="ExternalOutput").ap()

    with ExitStack() as st:
        c.st = st
        for s in ["gd", "sd", "cd", "pe", "dv", "ac", "po", "ex",
                  "gdx0", "gdx1", "sdx0", "sdx1", "gdw0", "gdw1", "gdw2", "gdw3", "sdr", "sdt", "sdt1", "sdp"]:
            setattr(c, s, Sem(st, nc, "s_" + s))
        c.kw_done = {}
        c.mm_done = {}
        c.x16_cast_done = {}
        c.x16_tp_done = {}
        c.wbuf_guard = None
        c.rope_scr_guard = None
        c.rope_rtmp_guard = None
        c.xres_used = {}
        c.logit_mm = {}
        c.zrow_used = 0
        c.zsb_used = 0
        c.y_mm = 0
        c.ysb_ac = 0
        c.xp_po_used = None
        c.last_out_sd = None
        c.xr_ok = {}
        c.xn_dv = {}

        c.blk = st.enter_context(nc.Block())
        c.idsb = st.enter_context(nc.sbuf_tensor("idsb", [128, 128], BF16))
        c.onsb = st.enter_context(nc.sbuf_tensor("onsb", [128, 128], BF16))
        c.hbsb = st.enter_context(nc.sbuf_tensor("hbsb", [24, 1], F32))
        c.gtsb = st.enter_context(nc.sbuf_tensor("gtsb", [1, 1], F32))
        c.tbsb = st.enter_context(nc.sbuf_tensor("tbsb", [24, 1], F32))
        c.idfsb = st.enter_context(nc.sbuf_tensor("idfsb", [24, 24], F32))
        c.onfsb = st.enter_context(nc.sbuf_tensor("onfsb", [1, 128], F32))
        c.epsb = st.enter_context(nc.sbuf_tensor("epsb", [128, 2], F32))
        c.scr = st.enter_context(nc.sbuf_tensor("scr", [128, 1024], BF16, side="right"))

        phase_consts(c)
        c.kvst = ExitStack()
        c.K16 = c.kvst.enter_context(nc.sbuf_tensor("K16", [128, NKV, T], BF16))
        phase_kv(c)
        c.qst = ExitStack()
        c.Q16 = c.qst.enter_context(nc.sbuf_tensor("Q16", [128, NH, TOWN], BF16, side="right"))
        phase_q(c)
        c.V16 = c.kvst.enter_context(nc.sbuf_tensor("V16", [128, 16, 1024], BF16))
        phase_v(c)
        emit_taps_p1(c)
        if upto == "p1":
            finish_dev(c)
            c.qst.close(); c.kvst.close()
            return nc, c
        phase_attn(c)
        if upto == "p2":
            finish_dev(c)
            c.qst.close(); c.kvst.close()
            return nc, c
        phase_tail(c)
        c.qst.close()
    return nc, c


def finish_dev(c):
    nc = c.nc
    if (blk := c.blk):
        @blk.sync
        def _(sy):
            for s in [c.gd, c.sd, c.cd, c.pe, c.dv, c.ac, c.po, c.ex,
                      c.gdx0, c.gdx1, c.sdx0, c.sdx1, c.gdw0, c.gdw1, c.gdw2, c.gdw3, c.sdr, c.sdt, c.sdt1, c.sdp]:
                if s.n:
                    sy.wait_ge(s.h, s.n)


def phase_consts(c):
    nc = c.nc
    if (blk := c.blk):
        @blk.scalar
        def _(sc):
            from concourse.hw_specs import get_activation_tables
            for idx, funcs in enumerate(
                    get_activation_tables(nc.m.arch).values()):
                if AF.Exp in funcs and AF.Ln in funcs:
                    sc.add_instruction(mybir.InstLoadActFuncSet(
                        name=nc.get_next_instruction_name(),
                        act_func_set_id=idx, ins=[], outs=[]))
                    break

        @blk.sync
        def _(sy):
            c.sd.inc(sy.dma_start(out=c.idsb[:, :], in_=c.ident), 16)
            c.sd.inc(sy.dma_start(out=c.onsb[:, :], in_=c.onesb), 16)
            c.sd.inc(sy.dma_start(out=c.hbsb[:, :], in_=c.hbase), 16)
            c.sd.inc(sy.dma_start(out=c.gtsb[:, :], in_=c.gate), 16)
            c.sd.inc(sy.dma_start(out=c.tbsb[:, :], in_=c.taub), 16)
            c.sd.inc(sy.dma_start(out=c.idfsb[:, :], in_=c.idf), 16)
            c.sd.inc(sy.dma_start(out=c.onfsb[:, :], in_=c.onesf), 16)
            c.sd.inc(sy.dma_start(out=c.epsb[:, :], in_=c.epsv), 16)
    c.consts_sd = c.sd.n


def emit_x16(c, X16, src, trows, half, xtmp, first_guard, nbuf=2):
    nc = c.nc
    ntc = trows // 128
    gdx = [c.gdx0, c.gdx1]
    sdx = [c.sdx0, c.sdx1]
    cast_done = {}
    tp0 = [sdx[0].n, sdx[1].n]
    if (blk := c.blk):
        @blk.gpsimd
        def _(g):
            for tc in range(ntc):
                b = tc % nbuf
                if tc >= nbuf:
                    g.wait_ge(sdx[b].h, tp0[b] + (tc // nbuf) * 16)
                elif tc == 0 and first_guard is not None:
                    for gd_ in first_guard:
                        g.wait_ge(*gd_)
                r0 = half * trows + tc * 128
                gdx[b].inc(g.dma_start(out=xtmp[:, b, :],
                                       in_=src[r0:r0 + 128, :]), 16)
                cast_done[tc] = gdx[b].n

        @blk.sync
        def _(sy):
            for tc in range(ntc):
                b = tc % nbuf
                sy.wait_ge(gdx[b].h, cast_done[tc])
                sdx[b].inc(sy.dma_start_transpose(
                    X16[:, :, tc * 128:(tc + 1) * 128],
                    xtmp[:, b, :].rearrange("p (c f) -> p c f", c=NF)), 16)


def emit_rope_mults(c, epair, opair, cos_sb, sin_sb, tok0, w, rtmp, par,
                    pe_guard, rtmp_guard=None):
    nc = c.nc
    if (blk := c.blk):
        @blk.vector
        def _(v):
            v.wait_ge(*pe_guard)
            v.wait_ge(*c.rope_tab_guard)
            if rtmp_guard is not None:
                v.wait_ge(*rtmp_guard)
            cs = cos_sb[:, tok0:tok0 + w]
            sn = sin_sb[:, tok0:tok0 + w]
            v.tensor_mul(rtmp[:, par, 0, 0:w], epair, cs)
            v.tensor_mul(rtmp[:, par, 1, 0:w], opair, sn)
            v.tensor_mul(rtmp[:, par, 2, 0:w], epair, sn)
            c.dv.inc(v.tensor_mul(rtmp[:, par, 3, 0:w], opair, cs), 1)
    return (c.dv.h, c.dv.n)


def emit_rope_combines(c, dst, chA, chB, tok0, w, rtmp, par, mul_guard):
    nc = c.nc
    if (blk := c.blk):
        @blk.gpsimd
        def _(p):
            p.wait_ge(*mul_guard)
            if c.rope_scr_guard is not None:
                p.wait_ge(*c.rope_scr_guard)
            p.tensor_sub(dst[0:64, chA, tok0:tok0 + w],
                         rtmp[0:64, par, 0, 0:w], rtmp[0:64, par, 1, 0:w])
            p.tensor_sub(c.scr[64:128, 0:w],
                         rtmp[64:128, par, 0, 0:w], rtmp[64:128, par, 1, 0:w])
            p.tensor_add(c.scr[0:64, 0:w],
                         rtmp[0:64, par, 2, 0:w], rtmp[0:64, par, 3, 0:w])
            c.po.inc(p.tensor_add(dst[64:128, chB, tok0:tok0 + w],
                                  rtmp[64:128, par, 2, 0:w],
                                  rtmp[64:128, par, 3, 0:w]), 1)
        @blk.sync
        def _(sy):
            sy.wait_ge(c.po.h, c.po.n)
            c.sdr.inc(sy.dma_start(out=dst[0:64, chB, tok0:tok0 + w],
                                   in_=c.scr[64:128, 0:w]), 16)
            c.sdr.inc(sy.dma_start(out=dst[64:128, chA, tok0:tok0 + w],
                                   in_=c.scr[0:64, 0:w]), 16)
            c.rope_scr_guard = (c.sdr.h, c.sdr.n)


def emit_wload(c, g, wbuf, buf, dram, r0, c0, ncols, fspan, nf0):
    src = dram[r0 + nf0 * 128: r0 + (nf0 + fspan) * 128, c0:c0 + ncols]
    src = src.rearrange("(a p) n -> p a n", p=128)
    sem = [c.gdw0, c.gdw1, c.gdw2, c.gdw3][buf]
    return sem.inc(g.dma_start(out=wbuf[:, buf, 0:fspan, 0:ncols], in_=src), 16)


def phase_kv(c):
    nc = c.nc
    FS = 4
    with ExitStack() as ph:
        X16h = ph.enter_context(nc.sbuf_tensor("X16h", [128, NF, 1024], BF16))
        xtmp = ph.enter_context(nc.sbuf_tensor("xtmp", [128, 2, D], BF16))
        wbuf = ph.enter_context(nc.sbuf_tensor("wbuf", [128, 2, 32, 256], BF16))
        cosf = ph.enter_context(nc.sbuf_tensor("cosf_sb", [128, T], F32))
        sinf = ph.enter_context(nc.sbuf_tensor("sinf_sb", [128, T], F32))
        rtmp = ph.enter_context(nc.sbuf_tensor("rtmp", [128, 2, 4, 1024], F32))
        psK = ph.enter_context(nc.psum_tensor("psK", [128, 2048], F32))
        psK2 = ph.enter_context(nc.psum_tensor("psK2", [128, 2048], F32))

        if (blk := c.blk):
            @blk.sync
            def _(sy):
                c.sd.inc(sy.dma_start(out=cosf[:, :], in_=c.cosf), 16)
                c.sd.inc(sy.dma_start(out=sinf[:, :], in_=c.sinf), 16)
        c.rope_tab_guard = (c.sd.h, c.sd.n)

        kps_guard = [None, None]
        phase_pe0 = c.pe.n
        unit = [0]
        pend_rope = []
        k_po0 = c.po.n
        for half in range(2):
            emit_x16(c, X16h, c.xb, 1024, half, xtmp,
                     [(c.pe.h, c.pe.n)] if (half or c.pe.n) else None)
            x16_ready = [(c.sdx0.h, c.sdx0.n), (c.sdx1.h, c.sdx1.n)]

            for r in range(4):
                po = (r % 2) * 1024
                pe0 = c.pe.n
                u0 = unit[0]
                if (blk := c.blk):
                    @blk.gpsimd
                    def _(g):
                        u = u0
                        if u >= 2:
                            g.wait_ge(c.pe.h, phase_pe0 + u - 1)
                        elif c.wbuf_guard is not None:
                            g.wait_ge(*c.wbuf_guard[u % 2])
                        emit_wload(c, g, wbuf, u % 2, c.wkv, 0, r * 256,
                                   256, 32, 0)
                        sem = [c.gdw0, c.gdw1][u % 2]
                        c.kw_done[0] = (sem.h, sem.n)

                    @blk.tensor
                    def _(t):
                        if r == 0:
                            for gx in x16_ready:
                                t.wait_ge(*gx)
                        u = u0
                        t.wait_ge(*c.kw_done[0])
                        if kps_guard[r % 2] is not None:
                            t.wait_ge(*kps_guard[r % 2])
                        for f in range(NF):
                            for n2 in range(2):
                                t.matmul(psK[:, po + n2 * 512:po + (n2 + 1) * 512],
                                         wbuf[:, u % 2, f, 0:128],
                                         X16h[:, f, n2 * 512:(n2 + 1) * 512],
                                         start=(f == 0), stop=(f == NF - 1))
                                mi = t.matmul(
                                    psK2[:, po + n2 * 512:po + (n2 + 1) * 512],
                                    wbuf[:, u % 2, f, 128:256],
                                    X16h[:, f, n2 * 512:(n2 + 1) * 512],
                                    start=(f == 0), stop=(f == NF - 1))
                        c.pe.inc(mi, 1)
                mg = emit_rope_mults(c, psK[:, po:po + 1024],
                                     psK2[:, po:po + 1024], cosf, sinf,
                                     half * 1024, 1024, rtmp, unit[0] % 2,
                                     (c.pe.h, pe0 + 1),
                                     (c.po.h, k_po0 + unit[0] - 1)
                                     if unit[0] >= 2 else None)
                kps_guard[r % 2] = mg
                pend_rope.append((2 * r, 2 * r + 1, half * 1024,
                                  unit[0] % 2, mg))
                unit[0] += 1
                if len(pend_rope) >= 2:
                    chA, chB, t0_, par, g_ = pend_rope.pop(0)
                    emit_rope_combines(c, c.K16, chA, chB, t0_, 1024, rtmp,
                                       par, g_)
        while pend_rope:
            chA, chB, t0_, par, g_ = pend_rope.pop(0)
            emit_rope_combines(c, c.K16, chA, chB, t0_, 1024, rtmp, par, g_)
        c.wbuf_guard = [(c.pe.h, c.pe.n)] * 4
        c.k_done_guards = [kps_guard[0], kps_guard[1],
                           (c.po.h, c.po.n), (c.sdr.h, c.sdr.n)]


def phase_v(c):
    nc = c.nc
    FS = 1
    with ExitStack() as ph:
        X16v = ph.enter_context(nc.sbuf_tensor("X16v", [128, NF, 1024], BF16))
        xtmp = ph.enter_context(nc.sbuf_tensor("xtmpv", [128, 1, D], BF16))
        wbuf = ph.enter_context(nc.sbuf_tensor("wbufv", [128, 4, FS, 512], BF16))
        psKv = ph.enter_context(nc.psum_tensor("psKv", [128, 2048], F32))
        psKv2 = ph.enter_context(nc.psum_tensor("psKv2", [128, 2048], F32))
        vps_guard = (c.dv.h, c.dv.n)
        nfb = NF // FS
        phase_pe0 = c.pe.n
        unit = [0]
        for half in range(2):
            emit_x16(c, X16v, c.xb, 1024, half, xtmp, [(c.pe.h, c.pe.n)],
                     nbuf=1)
            x16_ready = [(c.sdx0.h, c.sdx0.n), (c.sdx1.h, c.sdx1.n)]
            for vch in range(2):
                pe0 = c.pe.n
                u0 = unit[0]
                if (blk := c.blk):
                    @blk.gpsimd
                    def _(g):
                        for fb in range(nfb):
                            u = u0 + fb
                            if u >= 4:
                                g.wait_ge(c.pe.h, phase_pe0 + u - 3)
                            else:
                                g.wait_ge(*c.wbuf_guard[u % 4])
                            emit_wload(c, g, wbuf, u % 4, c.wkv, 0,
                                       1024 + vch * 512, 512, FS, fb * FS)
                            sem = [c.gdw0, c.gdw1, c.gdw2, c.gdw3][u % 4]
                            c.kw_done[fb] = (sem.h, sem.n)

                    @blk.tensor
                    def _(t):
                        if vch == 0:
                            for gx in x16_ready:
                                t.wait_ge(*gx)
                        for fb in range(nfb):
                            u = u0 + fb
                            t.wait_ge(*c.kw_done[fb])
                            if fb == 0 and vps_guard is not None:
                                t.wait_ge(*vps_guard)
                            for fi in range(FS):
                                f = fb * FS + fi
                                for tc4 in range(4):
                                    t.matmul(psKv[:, tc4 * 512:(tc4 + 1) * 512],
                                             X16v[:, f, tc4 * 128:(tc4 + 1) * 128],
                                             wbuf[:, u % 4, fi, 0:512],
                                             start=(f == 0), stop=(f == NF - 1))
                                    mi = t.matmul(
                                        psKv2[:, tc4 * 512:(tc4 + 1) * 512],
                                        X16v[:, f, (4 + tc4) * 128:(5 + tc4) * 128],
                                        wbuf[:, u % 4, fi, 0:512],
                                        start=(f == 0), stop=(f == NF - 1))
                            c.pe.inc(mi, 1)

                    @blk.scalar
                    def _(s):
                        s.wait_ge(c.pe.h, pe0 + nfb)
                        for tc4 in range(4):
                            s.copy(c.V16[:, half * 8 + tc4,
                                         vch * 512:(vch + 1) * 512],
                                   psKv[:, tc4 * 512:(tc4 + 1) * 512])
                            si = s.copy(c.V16[:, half * 8 + 4 + tc4,
                                              vch * 512:(vch + 1) * 512],
                                        psKv2[:, tc4 * 512:(tc4 + 1) * 512])
                        c.ac.inc(si, 1)
                vps_guard = (c.ac.h, c.ac.n)
                unit[0] += nfb
    c.wbuf_guard = [(c.pe.h, c.pe.n)] * 4
    c.v_done_guard = (c.ac.h, c.ac.n)


def phase_q(c):
    nc = c.nc
    FS = 8
    with ExitStack() as ph:
        X16q = ph.enter_context(nc.sbuf_tensor("X16q", [128, NF, TOWN], BF16))
        coso = ph.enter_context(nc.sbuf_tensor("coso_sb", [128, TOWN], F32))
        sino = ph.enter_context(nc.sbuf_tensor("sino_sb", [128, TOWN], F32))
        psA = ph.enter_context(nc.psum_tensor("psA", [128, 2048], F32))
        psB = ph.enter_context(nc.psum_tensor("psB", [128, 2048], F32))

        if (blk := c.blk):
            @blk.sync
            def _(sy):
                for gd_ in c.k_done_guards:
                    if gd_ is not None:
                        sy.wait_ge(*gd_)
                c.sd.inc(sy.dma_start(out=coso[:, :], in_=c.coso), 16)
                c.sd.inc(sy.dma_start(out=sino[:, :], in_=c.sino), 16)
        c.rope_tab_guard = (c.sd.h, c.sd.n)

        with ExitStack() as bst:
            xtmp = bst.enter_context(nc.sbuf_tensor("xtmpq", [128, 2, D], BF16))
            emit_x16(c, X16q, c.xq, TOWN, 0, xtmp,
                     [(c.pe.h, c.pe.n), (c.sdr.h, c.sdr.n),
                      (c.ac.h, c.ac.n), (c.dv.h, c.dv.n), (c.po.h, c.po.n)])
        x16_ready = [(c.sdx0.h, c.sdx0.n), (c.sdx1.h, c.sdx1.n)]
        rtmp = ph.enter_context(nc.sbuf_tensor("rtmpq", [128, 2, 4, 512], F32))
        wbuf = ph.enter_context(nc.sbuf_tensor("wbufq", [128, 2, 16, 256], BF16))

        qps_guard = [(c.dv.h, c.dv.n), (c.dv.h, c.dv.n)]
        phase_pe0 = c.pe.n
        pend_rope = []
        q_po0 = c.po.n
        for p in range(16):
            slot = p % 2
            po = slot * 1024
            pe0 = c.pe.n
            u0 = p * 2
            if (blk := c.blk):
                @blk.gpsimd
                def _(g):
                    for fh in range(2):
                        u = u0 + fh
                        if u >= 2:
                            g.wait_ge(c.pe.h, phase_pe0 + u - 1)
                        else:
                            g.wait_ge(*c.wbuf_guard[u % 2])
                        emit_wload(c, g, wbuf, u % 2, c.wq, 0, p * 256,
                                   256, 16, fh * 16)
                        sem = [c.gdw0, c.gdw1][u % 2]
                        c.kw_done[fh] = (sem.h, sem.n)

                @blk.tensor
                def _(t):
                    if p == 0:
                        for gx in x16_ready:
                            t.wait_ge(*gx)
                    for fh in range(2):
                        u = u0 + fh
                        t.wait_ge(*c.kw_done[fh])
                        if fh == 0 and qps_guard[slot] is not None:
                            t.wait_ge(*qps_guard[slot])
                        for fi in range(16):
                            f = fh * 16 + fi
                            for n2 in range(2):
                                t.matmul(psA[:, po + n2 * 512:po + (n2 + 1) * 512],
                                         wbuf[:, u % 2, fi, 0:128],
                                         X16q[:, f, n2 * 512:(n2 + 1) * 512],
                                         start=(f == 0), stop=(f == NF - 1))
                                mi = t.matmul(
                                    psB[:, po + n2 * 512:po + (n2 + 1) * 512],
                                    wbuf[:, u % 2, fi, 128:256],
                                    X16q[:, f, n2 * 512:(n2 + 1) * 512],
                                    start=(f == 0), stop=(f == NF - 1))
                        c.pe.inc(mi, 1)
            mg = None
            for th in range(2):
                par = (p * 2 + th) % 2
                u_ = p * 2 + th
                mg = emit_rope_mults(
                    c, psA[:, po + th * 512:po + (th + 1) * 512],
                    psB[:, po + th * 512:po + (th + 1) * 512],
                    coso, sino, th * 512, 512, rtmp, par,
                    (c.pe.h, pe0 + 2),
                    (c.po.h, q_po0 + u_ - 1) if u_ >= 2 else None)
                pend_rope.append((2 * p, 2 * p + 1, th * 512, par, mg))
                while len(pend_rope) >= 3:
                    chA, chB, t0_, par_, g_ = pend_rope.pop(0)
                    emit_rope_combines(c, c.Q16, chA, chB, t0_, 512, rtmp,
                                       par_, g_)
            qps_guard[slot] = mg
        while pend_rope:
            chA, chB, t0_, par_, g_ = pend_rope.pop(0)
            emit_rope_combines(c, c.Q16, chA, chB, t0_, 512, rtmp, par_, g_)
        c.wbuf_guard = [(c.pe.h, c.pe.n)] * 4


def emit_taps_p1(c):
    nc = c.nc
    c.tap_guard = None
    todo = [(n, t) for n, t in
            [("Q16", c.Q16), ("K16", c.K16), ("V16", c.V16)] if n in c.dbg]
    if not todo:
        return
    if (blk := c.blk):
        @blk.sync
        def _(sy):
            sy.wait_ge(c.sdr.h, c.sdr.n)
            sy.wait_ge(c.po.h, c.po.n)
            sy.wait_ge(c.ac.h, c.ac.n)
            for name, ten in todo:
                c.sd.inc(sy.dma_start(
                    out=c.dbg[name],
                    in_=ten[:, :, :].rearrange("p a b -> p (a b)")), 16)
    c.tap_guard = (c.sd.h, c.sd.n)


def phase_attn(c):
    nc = c.nc
    with ExitStack() as ph:
        masks = ph.enter_context(nc.sbuf_tensor("masks", [128, 16, 256], BF16))
        pt = ph.enter_context(nc.sbuf_tensor("pt", [128, 4, 4, 256], BF16))
        lntmp = ph.enter_context(nc.sbuf_tensor("lntmp", [128, 1024], F32))
        rrec = ph.enter_context(nc.sbuf_tensor("rrec", [128, 2, 1024], F32))
        scps = ph.enter_context(nc.psum_tensor("scps", [128, 2048], F32))
        psD = ph.enter_context(nc.psum_tensor("psD", [128, 1024], F32))
        psO = ph.enter_context(nc.psum_tensor("psO", [128, 1024], F32))

        if (blk := c.blk):
            @blk.gpsimd
            def _(g):
                g.wait_ge(c.pe.h, c.pe.n)
                c.gd.inc(g.dma_start(
                    out=masks[:, :, :],
                    in_=c.maskq.rearrange("p (a b) -> p a b", a=16)), 16)
        masks_ok = (c.gd.h, c.gd.n)

        work = []
        for h in range(NH):
            for s_ in range(4):
                for g in range(s_ + 1):
                    work.append((h, s_, g, g == s_))
        nwork = len(work)
        ex0 = c.ex.n
        ac0 = c.ac.n
        dv0 = c.dv.n
        sc_done = {}
        dp_done = {}
        dp_done_item = {}
        attn_sdr = c.sdr.n
        attn_po = c.po.n
        attn_ac = c.ac.n
        attn_dv = c.dv.n

        def recip_ac(h):
            return ac0 + 2 * h + 2

        def evac_dv(h):
            return dv0 + h + 1

        if (blk := c.blk):
            @blk.tensor
            def _(t):
                t.wait_ge(*masks_ok)
                t.wait_ge(c.sdr.h, attn_sdr)
                t.wait_ge(c.po.h, attn_po)
                t.wait_ge(c.ac.h, attn_ac)
                t.wait_ge(c.dv.h, attn_dv)
                pend = []

                def emit_dp(item):
                    wi2, (h, s_, g, last) = item
                    gkv = h >> 2
                    t.wait_ge(c.ex.h, ex0 + wi2 + 1)
                    if g == 0 and s_ == 0 and h >= 1:
                        t.wait_ge(c.ac.h, recip_ac(h - 1))
                        t.wait_ge(c.dv.h, evac_dv(h - 1))
                    so = s_ * 256
                    for j in range(4):
                        kb = 4 * g + j
                        st = (s_ % 2 == 0) and (kb == 0)
                        sp = last and (s_ % 2 == 1) and (j == 3)
                        t.matmul(psD[:, so:so + 256], c.onsb[:, :],
                                 pt[:, wi2 % 4, j, :], start=st, stop=sp,
                                 skip_group_check=True)
                        mi = t.matmul(psO[:, so:so + 256],
                                      c.V16[:, kb, gkv * 128:(gkv + 1) * 128],
                                      pt[:, wi2 % 4, j, :], start=st, stop=sp,
                                      skip_group_check=True)
                    c.pe.inc(mi, 1)
                    dp_done_item[wi2] = c.pe.n
                    if s_ == 3 and last:
                        dp_done[h] = c.pe.n

                for wi, (h, s_, g, last) in enumerate(work):
                    gkv = h >> 2
                    reg = (wi % 2) * 1024
                    if wi >= 2:
                        t.wait_ge(c.ex.h, ex0 + wi - 1)
                    for j in range(4):
                        kb = 4 * g + j
                        po2 = reg + j * 256
                        mi = t.matmul(scps[:, po2:po2 + 256],
                                      c.K16[:, gkv, kb * 128:(kb + 1) * 128],
                                      c.Q16[:, h, s_ * 256:(s_ + 1) * 256],
                                      start=True, stop=(not last),
                                      skip_group_check=True)
                        if last:
                            mi = t.matmul(scps[:, po2:po2 + 256],
                                          c.idsb[:, :],
                                          masks[:, s_ * 4 + j, :],
                                          start=False, stop=True,
                                          skip_group_check=True)
                    c.pe.inc(mi, 1)
                    sc_done[wi] = c.pe.n
                    pend.append((wi, (h, s_, g, last)))
                    if len(pend) >= 3:
                        emit_dp(pend.pop(0))
                while pend:
                    emit_dp(pend.pop(0))

            @blk.scalar
            def _(sc):
                for wi, (h, s_, g, last) in enumerate(work):
                    hp = h % 2
                    sc.wait_ge(c.pe.h, sc_done[wi])
                    if wi >= 4:
                        sc.wait_ge(c.pe.h, dp_done_item[wi - 4])
                    reg = (wi % 2) * 1024
                    a = sc.activation(
                        pt[:, wi % 4, :, :],
                        scps[:, reg:reg + 1024], AF.Exp, scale=SCALE)
                    c.ex.inc(a, 1)
                    if s_ == 3 and last:
                        sc.wait_ge(c.pe.h, dp_done[h])
                        c.ac.inc(sc.activation(lntmp[:, :], psD[:, :], AF.Ln), 1)
                        sc.wait_ge(c.ac.h, c.ac.n)
                        a2 = sc.activation(rrec[:, hp, :], lntmp[:, :],
                                           AF.Exp, scale=-1.0)
                        c.ac.inc(a2, 1)
                assert c.ac.n == ac0 + 2 * NH
                assert c.ex.n == ex0 + nwork

            @blk.vector
            def _(v):
                if c.tap_guard is not None:
                    v.wait_ge(*c.tap_guard)
                for h in range(NH):
                    hp = h % 2
                    v.wait_ge(c.ac.h, recip_ac(h))
                    c.dv.inc(v.tensor_mul(c.Q16[:, h, :], psO[:, :],
                                          rrec[:, hp, :]), 1)
                assert c.dv.n == dv0 + NH

        c.attn_done = {"pe": c.pe.n, "dv": c.dv.n, "ac": c.ac.n}

        d = c.dbg.get("AO")
        if d is not None:
            if (blk := c.blk):
                @blk.sync
                def _(sy):
                    sy.wait_ge(c.dv.h, c.dv.n)
                    c.sd.inc(sy.dma_start(
                        out=d, in_=c.Q16[:, :, :].rearrange("p a b -> p (a b)")), 16)


def phase_tail(c):
    nc = c.nc
    c.kvst.close()
    ad = c.attn_done
    with ExitStack() as ph:
        u = ph.enter_context(nc.sbuf_tensor("u", [128, 4, D], F32))
        wbr16 = ph.enter_context(nc.sbuf_tensor("wbr16", [128, 8, 1024], BF16))
        wlog16 = ph.enter_context(nc.sbuf_tensor("wlog16", [128, 32, 24], BF16))
        zsb = ph.enter_context(nc.sbuf_tensor("zsb", [128, 4, 24], F32))
        zrow = ph.enter_context(nc.sbuf_tensor("zrow", [24, 128], F32))
        usb = ph.enter_context(nc.sbuf_tensor("usb", [128, 4, 4], F32))
        vsb = ph.enter_context(nc.sbuf_tensor("vsb", [128, 4, 4], F32))
        tA = ph.enter_context(nc.sbuf_tensor("tA", [128, 4, 16], F32))
        tE = ph.enter_context(nc.sbuf_tensor("tE", [128, 4, 16], F32))
        ssb = ph.enter_context(nc.sbuf_tensor("ssb", [128, 4, 16], F32))
        csb = ph.enter_context(nc.sbuf_tensor("csb", [128, 4, 16], F32))
        hpre = ph.enter_context(nc.sbuf_tensor("hpre", [128, 4, 4], F32))
        hpost = ph.enter_context(nc.sbuf_tensor("hpost", [128, 4, 4], F32))
        srow = ph.enter_context(nc.sbuf_tensor("srow", [128, 4, 4], F32))
        sums = ph.enter_context(nc.sbuf_tensor("sums", [128, 16], F32))
        stat = ph.enter_context(nc.sbuf_tensor("stat", [128, 8], F32))
        gb = ph.enter_context(nc.sbuf_tensor("gb", [128, 1], F32))
        gb2 = ph.enter_context(nc.sbuf_tensor("gb2", [128, 1], F32))
        hb2 = ph.enter_context(nc.sbuf_tensor("hb2", [24, 1], F32))
        g1 = ph.enter_context(nc.sbuf_tensor("g1", [1, 4], F32))
        psW = ph.enter_context(nc.psum_tensor("psW", [128, 2048], F32))
        psY = ph.enter_context(nc.psum_tensor("psY", [128, 2048], F32))
        psL = psW[0:24, 1536:1664]
        psZt = psW[:, 1664:1688]

        if (blk := c.blk):
            @blk.gpsimd
            def _(g):
                g.wait_ge(c.pe.h, ad["pe"])
                g.wait_ge(c.dv.h, ad["dv"])
                c.gd.inc(g.dma_start(
                    out=wlog16[:, :, :],
                    in_=c.wlog.rearrange("(a p) n -> p a n", p=128)), 16)
                c.gd.inc(g.dma_start(
                    out=wbr16[:, :, :],
                    in_=c.wbr.rearrange("(a p) n -> p a n", p=128)), 16)

            @blk.vector
            def _(v):
                v.wait_ge(c.sd.h, c.sd.n)
                v.wait_ge(c.ac.h, ad["ac"])
                c.dv.inc(v.tensor_scalar_add(hb2[:, :], c.hbsb[:, :],
                                             -SINK_B / 10.0), 1)

            @blk.scalar
            def _(sc):
                sc.wait_ge(c.sd.h, c.sd.n)
                sc.wait_ge(c.dv.h, ad["dv"])
                sc.wait_ge(c.ac.h, ad["ac"])
                c.ac.inc(sc.activation(g1[:, 0:1], c.gtsb[:, :], AF.Exp,
                                       scale=-1.0), 1)
        if (blk := c.blk):
            @blk.vector
            def _(v):
                v.wait_ge(c.ac.h, c.ac.n)
                c.dv.inc(v.tensor_scalar_add(g1[:, 1:2], g1[:, 0:1], 1.0), 1)
            @blk.scalar
            def _(sc):
                sc.wait_ge(c.dv.h, c.dv.n)
                c.ac.inc(sc.activation(g1[:, 2:3], g1[:, 1:2], AF.Ln), 1)
                sc.wait_ge(c.ac.h, c.ac.n)
                c.ac.inc(sc.activation(g1[:, 3:4], g1[:, 2:3], AF.Exp,
                                       scale=-1.0), 1)
            @blk.tensor
            def _(t):
                t.wait_ge(c.ac.h, c.ac.n)
                c.pe.inc(t.matmul(psY[:, 0:1], c.onfsb[:, :], g1[:, 3:4],
                                  start=True, stop=True, skip_group_check=True), 1)
            @blk.vector
            def _(v):
                v.wait_ge(c.pe.h, c.pe.n)
                c.dv.inc(v.tensor_copy(gb[:, :], psY[:, 0:1]), 1)
                v.wait_ge(c.dv.h, c.dv.n)
                c.dv.inc(v.tensor_scalar(gb2[:, :], gb[:, :], -1.0, 2.0,
                                         OP.mult, OP.add), 1)
        tail_const_dv = c.dv.n
        tail_const_gd = c.gd.n

        xp16 = c.scr
        for sup in range(2):
            emit_super(c, sup, u, wbr16, wlog16,
                       xp16, zsb, zrow, usb, vsb, tA, tE, ssb,
                       csb, hpre, hpost, srow, sums, stat, gb, gb2, hb2,
                       psW, psL, psZt, psY, tail_const_dv, tail_const_gd)


def emit_super(c, sup, u, wbr16, wlog16, xp16,
               zsb, zrow, usb, vsb, tA, tE, ssb, csb, hpre,
               hpost, srow, sums, stat, gb, gb2, hb2, psW, psL, psZt, psY,
               tail_const_dv, tail_const_gd):
    nc = c.nc
    gdw = [c.gdw0, c.gdw1]
    sup_dv0 = c.dv.n
    sup_sdr0 = c.sdr.n
    sup_sd0 = c.sd.n
    stW = ExitStack()
    wob = stW.enter_context(nc.sbuf_tensor("wob_%d" % sup,
                                           [128, 3, 32, 256], BF16))

    wo_ld = {}
    pe0 = c.pe.n
    sup_po0 = c.po.n
    sup_ac0 = c.ac.n
    if (blk := c.blk):
        @blk.sync
        def _(sy):
            if sup == 0:
                sy.wait_ge(c.dv.h, tail_const_dv)
            else:
                sy.wait_ge(c.dv.h, sup_dv0)
                sy.wait_ge(c.ac.h, sup_ac0)
                sy.wait_ge(c.po.h, sup_po0)
            for tc in range(4):
                r0 = sup * 512 + tc * 128
                c.sdx0.inc(sy.dma_start(out=u[:, tc, :],
                                        in_=c.xq[r0:r0 + 128, :]), 16)
        upre_ok = (c.sdx0.h, c.sdx0.n)
    evac_dv0 = c.dv.n
    if (blk := c.blk):
        @blk.gpsimd
        def _(g):
            for g16 in range(16):
                if g16 < 3:
                    if sup == 0:
                        g.wait_ge(c.gd.h, tail_const_gd)
                        g.wait_ge(*c.wbuf_guard[g16 % 2])
                    else:
                        g.wait_ge(c.dv.h, sup_dv0)
                        g.wait_ge(c.sdp.h, c.sdp.n)
                        g.wait_ge(c.sd.h, sup_sd0)
                else:
                    g.wait_ge(c.pe.h, pe0 + (g16 - 2))
                sem3 = [c.gdw0, c.gdw1, c.gdw2][g16 % 3]
                sem3.inc(g.dma_start(
                    out=wob[:, g16 % 3, :, :],
                    in_=c.wo[:, g16 * 256:(g16 + 1) * 256]
                    .rearrange("(a p) n -> p a n", p=128)), 16)
                wo_ld[g16] = (sem3.h, sem3.n)

        @blk.tensor
        def _(t):
            if sup:
                t.wait_ge(c.dv.h, sup_dv0)
            else:
                t.wait_ge(c.dv.h, c.attn_done["dv"])
                t.wait_ge(c.ac.h, c.attn_done["ac"])
            for g16 in range(16):
                t.wait_ge(*wo_ld[g16])
                if g16 >= 2:
                    t.wait_ge(c.dv.h, evac_dv0 + (g16 - 1) * 4)
                reg = (g16 % 2) * 1024
                for d in range(32):
                    for tc in range(4):
                        mi = t.matmul(
                            psW[:, reg + tc * 256:reg + (tc + 1) * 256],
                            c.Q16[:, d, sup * 512 + tc * 128:
                                  sup * 512 + (tc + 1) * 128],
                            wob[:, g16 % 3, d, :],
                            start=(d == 0 and tc % 2 == 0), stop=(d == 31),
                            skip_group_check=True)
                c.pe.inc(mi, 1)

        @blk.vector
        def _(v):
            v.wait_ge(*upre_ok)
            assert c.dv.n == evac_dv0
            for g16 in range(16):
                v.wait_ge(c.pe.h, pe0 + g16 + 1)
                reg = (g16 % 2) * 1024
                for tc in range(4):
                    a = v.tensor_add(u[:, tc, g16 * 256:(g16 + 1) * 256],
                                     psW[:, reg + tc * 256:reg + (tc + 1) * 256],
                                     u[:, tc, g16 * 256:(g16 + 1) * 256])
                    c.dv.inc(a, 1)
    wo_pe = c.pe.n
    wo_evac_dv = c.dv.n

    stW.close()
    st4 = ExitStack()
    ptmp = st4.enter_context(nc.sbuf_tensor("ptmp_%d" % sup, [128, 1024], F32, side="right"))
    xp = st4.enter_context(nc.sbuf_tensor("xp_%d" % sup, [128, 1024], F32, side="right"))
    xpT = st4.enter_context(nc.sbuf_tensor("xpT_%d" % sup, [128, 8, 128], BF16, side="right"))
    st2 = ExitStack()
    xn16 = st2.enter_context(nc.sbuf_tensor("xn16_%d" % sup, [128, 2, D], BF16))
    xnT = st2.enter_context(nc.sbuf_tensor("xnT_%d" % sup, [128, 2, 32, 128], BF16))

    radd_dv = {}
    if (blk := c.blk):
        @blk.vector
        def _(v):
            v.wait_ge(c.dv.h, wo_evac_dv)
            for tc in range(4):
                c.dv.inc(v.reduce_sum(stat[:, tc:tc + 1].unsqueeze(1),
                                      u[:, tc, :].unsqueeze(1), AX.X), 1)
                radd_dv[tc] = c.dv.n

    if (blk := c.blk):
        @blk.scalar
        def _(sc):
            for tc in range(4):
                if tc:
                    sc.wait_ge(c.ac.h, c.ac.n)
                sc.wait_ge(c.dv.h, radd_dv[tc])
                c.ac.inc(sc.activation(xn16[:, tc % 2, :], u[:, tc, :],
                                       AF.Square,
                                       accum_out=stat[:, 4 + tc:5 + tc]), 1)
        sq_ac = c.ac.n

        @blk.vector
        def _(v):
            v.wait_ge(c.ac.h, sq_ac)
            c.dv.inc(v.tensor_scalar_mul(stat[:, 0:4], stat[:, 0:4], 1.0 / D), 1)
            v.wait_ge(c.dv.h, c.dv.n)
            c.dv.inc(v.tensor_mul(sums[:, 0:4], stat[:, 0:4], stat[:, 0:4]), 1)
            v.wait_ge(c.dv.h, c.dv.n)
            c.dv.inc(v.scalar_tensor_tensor(sums[:, 4:8], stat[:, 4:8],
                                            1.0 / D, sums[:, 0:4],
                                            OP.mult, OP.subtract), 1)
        var_dv = c.dv.n

        @blk.scalar
        def _(sc):
            sc.wait_ge(c.dv.h, var_dv)
            c.ac.inc(sc.activation(sums[:, 4:8], sums[:, 4:8], AF.Ln,
                                   bias=c.epsb[:, 0:1]), 1)
            sc.wait_ge(c.ac.h, c.ac.n)
            c.ac.inc(sc.activation(sums[:, 0:4], sums[:, 4:8], AF.Exp,
                                   scale=-0.5), 1)
        rstd_ac = c.ac.n

        @blk.vector
        def _(v):
            v.wait_ge(c.ac.h, rstd_ac)
            for tc in range(4):
                c.dv.inc(v.tensor_scalar(u[:, tc, :], u[:, tc, :],
                                         stat[:, tc:tc + 1],
                                         sums[:, tc:tc + 1],
                                         OP.subtract, OP.mult), 1)
                radd_dv[tc] = c.dv.n

        @blk.scalar
        def _(sc):
            for tc in range(4):
                if tc:
                    sc.wait_ge(c.ac.h, c.ac.n)
                sc.wait_ge(c.dv.h, radd_dv[tc])
                c.ac.inc(sc.activation(xn16[:, tc % 2, :], u[:, tc, :],
                                       AF.Square,
                                       accum_out=stat[:, 4 + tc:5 + tc]), 1)
        rsq_ac = c.ac.n

        @blk.scalar
        def _(sc):
            sc.wait_ge(c.ac.h, rsq_ac)
            c.ac.inc(sc.activation(sums[:, 4:8], stat[:, 4:8], AF.Ln,
                                   bias=c.epsb[:, 1:2], scale=1.0 / D), 1)
            sc.wait_ge(c.ac.h, c.ac.n)
            c.ac.inc(sc.activation(sums[:, 8:12], sums[:, 4:8], AF.Exp,
                                   scale=-0.5), 1)
        rms_ac = c.ac.n

        @blk.vector
        def _(v):
            v.wait_ge(c.ac.h, rms_ac)
            sdt_s6 = [c.sdt.n, c.sdt1.n]
            for tc in range(4):
                if tc >= 2:
                    v.wait_ge([c.sdt, c.sdt1][tc % 2].h, sdt_s6[tc % 2] + 16)
                c.dv.inc(v.tensor_scalar_mul(xn16[:, tc % 2, :], u[:, tc, :],
                                             sums[:, 8 + tc:9 + tc]), 1)
                c.xn_dv[tc] = c.dv.n

    for tc in range(4):
        if (blk := c.blk):
            @blk.sync
            def _(sy):
                sy.wait_ge(c.dv.h, c.xn_dv[tc])
                prev = c.logit_mm.get(tc - 2 if tc >= 2 else "last")
                if prev:
                    sy.wait_ge(c.pe.h, prev)
                [c.sdt, c.sdt1][tc % 2].inc(sy.dma_start_transpose(
                    xnT[:, tc % 2, :, :],
                    xn16[:, tc % 2, :].rearrange("p (a b) -> p a b", a=32)), 16)
            xnT_ok = ([c.sdt, c.sdt1][tc % 2].h, [c.sdt, c.sdt1][tc % 2].n)

            @blk.tensor
            def _(t):
                t.wait_ge(*xnT_ok)
                if c.zrow_used:
                    t.wait_ge(c.dv.h, c.zrow_used)
                for f in range(32):
                    mi = t.matmul(psL, wlog16[:, f, :],
                                  xnT[:, tc % 2, f, :],
                                  start=(f == 0), stop=(f == 31),
                                  skip_group_check=True)
                c.pe.inc(mi, 1)
                c.logit_mm[tc] = c.pe.n
                c.logit_mm["last"] = c.pe.n

            @blk.vector
            def _(v):
                v.wait_ge(c.pe.h, c.logit_mm[tc])
                c.dv.inc(v.tensor_scalar(zrow[:, :], psL,
                                         hb2[:, :], c.tbsb[:, :],
                                         OP.add, OP.mult), 1)
                c.zrow_used = c.dv.n

            @blk.tensor
            def _(t):
                t.wait_ge(c.dv.h, c.zrow_used)
                if c.zsb_used:
                    t.wait_ge(c.dv.h, c.zsb_used)
                c.pe.inc(t.transpose(psZt, zrow[:, :],
                                     c.idfsb[0:24, 0:24]), 1)
                zt_pe = c.pe.n

            @blk.vector
            def _(v):
                v.wait_ge(c.pe.h, c.pe.n)
                c.dv.inc(v.tensor_copy(zsb[:, tc, :], psZt), 1)
                c.zsb_used = c.dv.n
    zall_dv = c.dv.n
    st2.close()

    if (blk := c.blk):
        for off, dst in ((16, hpre), (20, hpost)):
            @blk.scalar
            def _(sc, off=off, dst=dst):
                sc.wait_ge(c.dv.h, c.dv.n)
                sc.wait_ge(c.ac.h, c.ac.n)
                c.ac.inc(sc.activation(srow[:, :, :], zsb[:, :, off:off + 4],
                                       AF.Exp), 1)

            @blk.vector
            def _(v, off=off, dst=dst):
                v.wait_ge(c.ac.h, c.ac.n)
                c.dv.inc(v.reduce_sum(stat[:, 0:4].rearrange("p (c o) -> p c o", c=4),
                                      srow[:, :, :], AX.X), 1)

            @blk.scalar
            def _(sc, off=off, dst=dst):
                sc.wait_ge(c.dv.h, c.dv.n)
                c.ac.inc(sc.activation(stat[:, 4:8], stat[:, 0:4], AF.Ln), 1)
                sc.wait_ge(c.ac.h, c.ac.n)
                c.ac.inc(sc.activation(stat[:, 0:4], stat[:, 4:8], AF.Exp,
                                       scale=-1.0), 1)

            @blk.vector
            def _(v, off=off, dst=dst):
                v.wait_ge(c.ac.h, c.ac.n)
                c.dv.inc(v.tensor_mul(
                    dst[:, :, :], srow[:, :, :],
                    stat[:, 0:4].rearrange("p (c o) -> p c o", c=4)
                    .broadcast_to([128, 4, 4])), 1)
    h_dv = c.dv.n

    yreg = [psW[:, 0:1024], psW[:, 1024:2048], psY[:, 0:1024], psY[:, 1024:2048]]
    y_pe = {}
    for tc in range(4):
        if (blk := c.blk):
            @blk.gpsimd
            def _(p):
                p.wait_ge(c.dv.h, h_dv)
                if c.xp_po_used is not None:
                    p.wait_ge(c.sdp.h, c.xp_po_used)
                x1c = u[:, tc, :].rearrange("p (n k) -> p n k", n=4)
                c.po.inc(p.tensor_mul(
                    xp[:, :], x1c[:, 0, :],
                    hpre[:, tc, 0:1].broadcast_to([128, 1024])), 1)
                for n in range(1, 4):
                    p.wait_ge(c.po.h, c.po.n)
                    c.po.inc(p.tensor_mul(
                        ptmp[:, :], x1c[:, n, :],
                        hpre[:, tc, n:n + 1].broadcast_to([128, 1024])), 1)
                    p.wait_ge(c.po.h, c.po.n)
                    c.po.inc(p.tensor_add(xp[:, :], xp[:, :], ptmp[:, :]), 1)
                p.wait_ge(c.po.h, c.po.n)
                c.po.inc(p.tensor_copy(xp16[:, :], xp[:, :]), 1)
            xp_po = c.po.n

            @blk.sync
            def _(sy):
                sy.wait_ge(c.po.h, xp_po)
                if c.y_mm:
                    sy.wait_ge(c.pe.h, c.y_mm)
                c.sdp.inc(sy.dma_start_transpose(
                    xpT[:, :, :],
                    xp16[:, :].rearrange("p (a b) -> p a b", a=8)), 16)
                c.xp_po_used = c.sdp.n
            xpT_ok = c.sdp.n

            @blk.tensor
            def _(t):
                t.wait_ge(c.sdp.h, xpT_ok)
                for co in range(2):
                    for ci in range(8):
                        mi = t.matmul(yreg[tc][:, co * 512:(co + 1) * 512],
                                      xpT[:, ci, :],
                                      wbr16[:, ci, co * 512:(co + 1) * 512],
                                      start=(ci == 0), stop=(ci == 7),
                                      skip_group_check=True)
                    c.pe.inc(mi, 1)
                c.y_mm = c.pe.n
                y_pe[tc] = c.pe.n

    if (blk := c.blk):
        zr = zsb[:, :, 0:16].rearrange("p c (i j) -> p c i j", i=4)
        zrT = zr.transpose([0, 1, 3, 2])
        sk0_dv = c.dv.n

        for it in range(50):
            @blk.vector
            def _(v, it=it):
                if it == 0:
                    v.wait_ge(c.dv.h, sk0_dv)
                else:
                    v.wait_ge(c.ac.h, c.ac.n)
                if it == 0:
                    c.dv.inc(v.tensor_copy(
                        tA[:, :, :].rearrange("p c (i j) -> p c i j", i=4), zr), 1)
                else:
                    c.dv.inc(v.tensor_sub(
                        tA[:, :, :].rearrange("p c (i j) -> p c i j", i=4),
                        zr, vsb[:, :, :].unsqueeze(2).broadcast_to(
                            [128, 4, 4, 4])), 1)

            @blk.scalar
            def _(sc, it=it):
                sc.wait_ge(c.dv.h, c.dv.n)
                c.ac.inc(sc.activation(tE[:, :, :], tA[:, :, :], AF.Exp), 1)

            @blk.vector
            def _(v, it=it):
                v.wait_ge(c.ac.h, c.ac.n)
                c.dv.inc(v.reduce_sum(
                    usb[:, :, :],
                    tE[:, :, :].rearrange("p c (i j) -> p c i j", i=4), AX.X), 1)

            @blk.scalar
            def _(sc, it=it):
                sc.wait_ge(c.dv.h, c.dv.n)
                c.ac.inc(sc.activation(usb[:, :, :], usb[:, :, :], AF.Ln), 1)

            @blk.vector
            def _(v, it=it):
                v.wait_ge(c.ac.h, c.ac.n)
                c.dv.inc(v.tensor_sub(
                    tA[:, :, :].rearrange("p c (j i) -> p c j i", j=4),
                    zrT, usb[:, :, :].unsqueeze(2).broadcast_to(
                        [128, 4, 4, 4])), 1)

            @blk.scalar
            def _(sc, it=it):
                sc.wait_ge(c.dv.h, c.dv.n)
                c.ac.inc(sc.activation(tE[:, :, :], tA[:, :, :], AF.Exp), 1)

            @blk.vector
            def _(v, it=it):
                v.wait_ge(c.ac.h, c.ac.n)
                c.dv.inc(v.reduce_sum(
                    vsb[:, :, :],
                    tE[:, :, :].rearrange("p c (j i) -> p c j i", j=4), AX.X), 1)

            @blk.scalar
            def _(sc, it=it):
                sc.wait_ge(c.dv.h, c.dv.n)
                c.ac.inc(sc.activation(vsb[:, :, :], vsb[:, :, :], AF.Ln), 1)

        @blk.vector
        def _(v):
            v.wait_ge(c.ac.h, c.ac.n)
            c.dv.inc(v.tensor_sub(
                tA[:, :, :].rearrange("p c (i j) -> p c i j", i=4),
                zr, usb[:, :, :].unsqueeze(3).broadcast_to([128, 4, 4, 4])), 1)
            v.wait_ge(c.dv.h, c.dv.n)
            c.dv.inc(v.tensor_sub(
                tA[:, :, :].rearrange("p c (i j) -> p c i j", i=4),
                tA[:, :, :].rearrange("p c (i j) -> p c i j", i=4),
                vsb[:, :, :].unsqueeze(2).broadcast_to([128, 4, 4, 4])), 1)

        @blk.scalar
        def _(sc):
            sc.wait_ge(c.dv.h, c.dv.n)
            c.ac.inc(sc.activation(ssb[:, :, :], tA[:, :, :], AF.Exp), 1)

        @blk.vector
        def _(v):
            v.wait_ge(c.ac.h, c.ac.n)
            c.dv.inc(v.tensor_scalar_mul(csb[:, :, :], ssb[:, :, :],
                                         gb[:, 0:1]), 1)
            v.wait_ge(c.dv.h, c.dv.n)
            c.dv.inc(v.tensor_scalar_add(
                csb[:, :, 0:16:5].rearrange("p c i -> p c i"),
                csb[:, :, 0:16:5].rearrange("p c i -> p c i"),
                gb2[:, 0:1]), 1)
    sink_dv = c.dv.n

    st3 = ExitStack()
    mixt = st3.enter_context(nc.sbuf_tensor("mixt_%d" % sup, [128, 4, 1024], F32))
    for tc in range(4):
        if (blk := c.blk):
            def mix_x(eng, sem, i, tc=tc, pool=False):
                x1c = u[:, tc, :].rearrange("p (n k) -> p n k", n=4)
                if pool:
                    sem.inc(eng.tensor_mul(
                        mixt[:, i, :], x1c[:, 0, :],
                        csb[:, tc, 4 * i:4 * i + 1].broadcast_to([128, 1024])), 1)
                    for n in range(1, 4):
                        eng.wait_ge(sem.h, sem.n)
                        sem.inc(eng.tensor_mul(
                            ptmp[:, :], x1c[:, n, :],
                            csb[:, tc, 4 * i + n:4 * i + n + 1]
                            .broadcast_to([128, 1024])), 1)
                        eng.wait_ge(sem.h, sem.n)
                        sem.inc(eng.tensor_add(mixt[:, i, :], mixt[:, i, :],
                                               ptmp[:, :]), 1)
                    return
                sem.inc(eng.tensor_scalar_mul(mixt[:, i, :], x1c[:, 0, :],
                                              csb[:, tc, 4 * i:4 * i + 1]), 1)
                for n in range(1, 4):
                    eng.wait_ge(sem.h, sem.n)
                    sem.inc(eng.scalar_tensor_tensor(
                        mixt[:, i, :], x1c[:, n, :],
                        csb[:, tc, 4 * i + n:4 * i + n + 1],
                        mixt[:, i, :], OP.mult, OP.add), 1)

            def mix_y(v, i, tc=tc):
                v.wait_ge(c.dv.h, c.dv.n)
                c.dv.inc(v.scalar_tensor_tensor(
                    mixt[:, i, :], yreg[tc][:, :],
                    hpost[:, tc, i:i + 1],
                    mixt[:, i, :], OP.mult, OP.add,
                    accum_out=sums[:, i:i + 1]), 1)

            @blk.gpsimd
            def _(p):
                p.wait_ge(c.dv.h, sink_dv)
                if c.last_out_sd is not None:
                    p.wait_ge(c.sd.h, c.last_out_sd)
                mix_x(p, c.po, 0, pool=True)
            mixp_po = c.po.n

            @blk.vector
            def _(v):
                v.wait_ge(c.dv.h, sink_dv)
                v.wait_ge(c.pe.h, y_pe[tc])
                if c.last_out_sd is not None:
                    v.wait_ge(c.sd.h, c.last_out_sd)
                mix_x(v, c.dv, 1)
                mix_y(v, 1)
                mix_x(v, c.dv, 2)
                mix_y(v, 2)
                mix_x(v, c.dv, 3)
                mix_y(v, 3)
                v.wait_ge(c.po.h, mixp_po)
                mix_y(v, 0)
                v.wait_ge(c.dv.h, c.dv.n)
                c.dv.inc(v.reduce_sum(stat[:, 0:1].unsqueeze(1),
                                      sums[:, 0:4].unsqueeze(1), AX.X), 1)
            mix_dv = c.dv.n

            @blk.scalar
            def _(sc):
                sc.wait_ge(c.dv.h, mix_dv)
                c.ac.inc(sc.activation(u[:, tc, :],
                                       mixt[:, :, :].rearrange("p a b -> p (a b)"),
                                       AF.Square,
                                       accum_out=stat[:, 1:2]), 1)
            lnsq_ac = c.ac.n

            @blk.vector
            def _(v):
                v.wait_ge(c.ac.h, lnsq_ac)
                c.dv.inc(v.tensor_scalar_mul(stat[:, 2:3], stat[:, 0:1],
                                             1.0 / D), 1)
                v.wait_ge(c.dv.h, c.dv.n)
                c.dv.inc(v.tensor_mul(stat[:, 3:4], stat[:, 2:3],
                                      stat[:, 2:3]), 1)
                v.wait_ge(c.dv.h, c.dv.n)
                c.dv.inc(v.scalar_tensor_tensor(stat[:, 4:5], stat[:, 1:2],
                                                1.0 / D, stat[:, 3:4],
                                                OP.mult, OP.subtract), 1)

            @blk.scalar
            def _(sc):
                sc.wait_ge(c.dv.h, c.dv.n)
                c.ac.inc(sc.activation(stat[:, 5:6], stat[:, 4:5], AF.Ln,
                                       bias=c.epsb[:, 0:1]), 1)
                sc.wait_ge(c.ac.h, c.ac.n)
                c.ac.inc(sc.activation(stat[:, 6:7], stat[:, 5:6], AF.Exp,
                                       scale=-0.5), 1)

            @blk.vector
            def _(v):
                v.wait_ge(c.ac.h, c.ac.n)
                for i in range(4):
                    a = v.tensor_scalar(mixt[:, i, :], mixt[:, i, :],
                                        stat[:, 2:3], stat[:, 6:7],
                                        OP.subtract, OP.mult)
                c.dv.inc(a, 1)
            out_dv = c.dv.n

            @blk.sync
            def _(sy):
                sy.wait_ge(c.dv.h, out_dv)
                r0 = sup * 512 + tc * 128
                c.sd.inc(sy.dma_start(
                    out=c.out[r0:r0 + 128, :],
                    in_=mixt[:, :, :].rearrange("p a b -> p (a b)")), 16)
                c.last_out_sd = c.sd.n
    st2.close()
    st3.close()
    st4.close()
    if sup == 1:
        if (blk := c.blk):
            @blk.sync
            def _(sy):
                sy.wait_ge(c.sd.h, c.sd.n)



def rope_perm_cols(nheads):
    idx = []
    for p in range(nheads // 2):
        for eo in range(2):
            for a in range(2):
                h = 2 * p + a
                for i in range(64):
                    idx.append(h * 128 + 2 * i + eo)
    return np.array(idx)


def prep_core_inputs(inputs, core):
    b, ch = core >> 1, core & 1
    J = JSETS[ch]
    x = np.asarray(inputs["L0_attn_x"])
    w_qkv = np.asarray(inputs["L0_attn_w_qkv"])
    w_o = np.asarray(inputs["L0_attn_w_o"])

    own_rows = np.concatenate([np.arange(j * 256, (j + 1) * 256) for j in J])
    xb = np.ascontiguousarray(x[b])
    xq = np.ascontiguousarray(x[b][own_rows])

    wq = np.ascontiguousarray(w_qkv[:, rope_perm_cols(NH)])
    kperm = 4096 + rope_perm_cols(NKV)
    wkv = np.ascontiguousarray(
        np.concatenate([w_qkv[:, kperm], w_qkv[:, 5120:6144]], axis=1))

    inv_freq = 1.0 / np.power(np.float64(10000.0),
                               np.arange(0, HD, 2, dtype=np.float64) / HD)
    angles = np.arange(T, dtype=np.float64)[:, None] * inv_freq[None, :]
    cos_t = np.cos(angles).astype(np.float32).T
    sin_t = np.sin(angles).astype(np.float32).T
    cosf = np.ascontiguousarray(np.vstack([cos_t, cos_t]))
    sinf = np.ascontiguousarray(np.vstack([sin_t, sin_t]))
    coso = np.ascontiguousarray(cosf[:, own_rows])
    sino = np.ascontiguousarray(sinf[:, own_rows])

    maskq = np.zeros((128, 4096), np.float32)
    for k, j in enumerate(J):
        nat = 2 * j + 2
        w0 = 4 * (k + 1) - 4
        qg = np.arange(j * 256, (j + 1) * 256)
        for wi in range(4):
            kb = w0 + wi
            kg = np.arange(kb * 128, (kb + 1) * 128)
            m = np.where(kg[:, None] <= qg[None, :], 0.0, NEG).astype(np.float32)
            maskq[:, (k * 4 + wi) * 256:(k * 4 + wi + 1) * 256] = m

    hbase = np.concatenate([
        np.asarray(inputs["L0_ffn_H_res_base"]).reshape(16),
        np.asarray(inputs["L0_ffn_H_pre_base"]).reshape(4),
        np.asarray(inputs["L0_ffn_H_post_base"]).reshape(4)]).astype(
            np.float32).reshape(24, 1)
    wlog = np.ascontiguousarray(np.concatenate(
        [np.asarray(inputs["L0_ffn_W_res"]), np.asarray(inputs["L0_ffn_W_pre"]),
         np.asarray(inputs["L0_ffn_W_post"])], axis=0).T.astype(np.float32))

    return {
        "xb": xb, "xq": xq, "wq": wq, "wkv": wkv,
        "wo": np.ascontiguousarray(w_o),
        "wbr": np.ascontiguousarray(np.asarray(inputs["L0_ffn_W_branch"])),
        "wlog": wlog, "cosf": cosf, "sinf": sinf, "coso": coso, "sino": sino,
        "maskq": maskq,
        "ident": np.eye(128, dtype=BF),
        "onesb": np.ones((128, 128), dtype=BF),
        "hbase": hbase,
        "gate": np.asarray(inputs["L0_ffn_gate"]).reshape(1, 1).astype(np.float32),
        "taub": np.concatenate([np.full(16, 10.0, np.float32),
                                np.ones(8, np.float32)]).reshape(24, 1),
        "idf": np.eye(24, dtype=np.float32),
        "onesf": np.ones((1, 128), np.float32),
        "epsv": np.tile(np.array([[LN_EPS, RMS_EPS]], np.float32), (128, 1)),
    }


def assemble_output(results):
    out = np.zeros((4, T, D), np.float32)
    for core, r in enumerate(results):
        b, ch = core >> 1, core & 1
        o = r["out"]
        for k, j in enumerate(JSETS[ch]):
            out[b, j * 256:(j + 1) * 256] = o[k * 256:(k + 1) * 256]
    return out




from concourse.bass_utils import run_bass_kernel_spmd


def kernel(L0_attn_x, L0_attn_w_qkv, L0_attn_w_o, L0_ffn_W_res, L0_ffn_W_pre,
           L0_ffn_W_post, L0_ffn_H_res_base, L0_ffn_H_pre_base,
           L0_ffn_H_post_base, L0_ffn_gate, L0_ffn_W_branch):
    inputs = {
        "L0_attn_x": L0_attn_x, "L0_attn_w_qkv": L0_attn_w_qkv,
        "L0_attn_w_o": L0_attn_w_o, "L0_ffn_W_res": L0_ffn_W_res,
        "L0_ffn_W_pre": L0_ffn_W_pre, "L0_ffn_W_post": L0_ffn_W_post,
        "L0_ffn_H_res_base": L0_ffn_H_res_base,
        "L0_ffn_H_pre_base": L0_ffn_H_pre_base,
        "L0_ffn_H_post_base": L0_ffn_H_post_base,
        "L0_ffn_gate": L0_ffn_gate, "L0_ffn_W_branch": L0_ffn_W_branch,
    }
    nc, c = build()
    nc.finalize()
    in_maps = [prep_core_inputs(inputs, core) for core in range(8)]
    res = run_bass_kernel_spmd(nc, in_maps, core_ids=list(range(8)))
    return assemble_output(res.results)
